# version 13
# baseline (speedup 1.0000x reference)
import numpy as np
import ml_dtypes
import concourse.bass as bass
import concourse.mybir as mybir
from concourse.bass_utils import run_bass_kernel_spmd
from contextlib import ExitStack

F32 = mybir.dt.float32
BF16 = mybir.dt.bfloat16
I32 = mybir.dt.int32
AF = mybir.ActivationFunctionType
ALU = mybir.AluOpType

E = 2048
KE = 16
VH = 32
KH = 16
C = 64
SPAN = 2048
SUB = 512
NSUB = SPAN // SUB
NCS = SUB // C
NSC = SPAN // C
NQKV = 8192
NZ = 4096
HFFN = 16384
HALF_H = 8192
LN_DKI = float(np.log(128.0 ** -0.5))
QTHR = 0.5009765625
F32EPS = float(np.finfo(np.float32).eps)


class Rec:
    def __init__(self):
        self.items = {k: [] for k in ("sync", "gpsimd", "vector", "scalar", "tensor")}

    def emit(self, block):
        def mk(name):
            def body(eng):
                stack = []
                for it in self.items[name]:
                    kind = it[0]
                    if kind == "wait":
                        eng.wait_ge(it[1], it[2])
                    elif kind == "waitreg":
                        eng.wait_ge(it[1], eng.value_load(it[2]))
                    elif kind == "iff":
                        g = eng.If(eng.value_load(it[1]))
                        g.__enter__()
                        stack.append(g)
                    elif kind == "endif":
                        stack.pop().__exit__(None, None, None)
                    elif kind == "poll":
                        sem, flgap, srcap, nonceap, tgt0 = it[1:]
                        nreg = eng.value_load(nonceap)
                        self._pollid = getattr(self, "_pollid", 0) + 1
                        with eng.register(f"ptgt{self._pollid}") as tgt:
                            eng.reg_mov(tgt, tgt0 + 16)
                            with eng.While(lambda: eng.value_load(flgap) != nreg):
                                eng.nop(cycle_cnt=2000)
                                eng.dma_start(out=flgap, in_=srcap).then_inc(sem, 16)
                                eng.wait_ge(sem, tgt)
                                eng.reg_add(tgt, tgt, 16)
                    else:
                        d = it[1]
                        ins = getattr(eng, d["meth"])(*d["args"], **d["kwargs"])
                        for sem, n in d["incs"]:
                            ins.then_inc(sem, n)
            return body
        block.sync(mk("sync"))
        block.gpsimd(mk("gpsimd"))
        block.vector(mk("vector"))
        block.scalar(mk("scalar"))
        block.tensor(mk("tensor"))


SELF = {}
LASTC = {}
SELF_ENGS = ("vector", "scalar", "gpsimd")


class Eng:
    def __init__(self, rec, name):
        self._rec, self._name = rec, name

    def __getattr__(self, meth):
        rec, name = self._rec, self._name

        def call(*args, **kwargs):
            if name in SELF_ENGS and SELF:
                lastc = LASTC.get(name)
                if lastc is not None:
                    if lastc["sync"] is None:
                        SELF["ctr"].inc_d(lastc)
                    rec.items[name].append(("wait",) + lastc["sync"])
            entry = ["op", {"meth": meth, "args": args, "kwargs": kwargs,
                            "incs": [], "eng": name, "sync": None}]
            rec.items[name].append(entry)
            if name in SELF_ENGS and SELF and meth != "dma_start":
                LASTC[name] = entry[1]
            return entry
        return call


class Ctr:

    def __init__(self, es, nc, name):
        self.es, self.nc, self.name = es, nc, name
        self.sub = {}

    def _get(self, eng):
        if eng not in self.sub:
            self.sub[eng] = [self.es.enter_context(self.nc.semaphore(f"{self.name}_{eng[:2]}")), 0]
        return self.sub[eng]

    def inc_d(self, d, n=1):
        s = self._get(d["eng"])
        s[1] += n
        d["incs"].append((s[0], n))
        if d["sync"] is None:
            d["sync"] = (s[0], s[1])

    def inc(self, entry, n=1):
        self.inc_d(entry[1], n)

    def dma(self, entry):
        self.inc(entry, 16)

    @property
    def n(self):
        return sum(s[1] for s in self.sub.values())

    def snap(self):
        return {e: s[1] for e, s in self.sub.items()}

    def sem_for(self, eng):
        return self._get(eng)[0]

    def n_of(self, eng):
        return self._get(eng)[1]


def build(dbg=False):
    nc = bass.Bass("TRN2", target_bir_lowering=False)
    es = ExitStack()
    D = lambda name, sh, dt=F32: nc.dram_tensor(name, sh, dt, kind="ExternalInput").ap()
    xT = D("xT", [E, SPAN + 3])
    wqkvz = D("wqkvz", [E, 12288])
    wba = D("wba", [E, 64])
    xPT = D("xPT", [E, 4096 + 3])
    wkvP = D("wkvP", [E, 3072])
    wbaP = D("wbaP", [E, 64])
    convwP = D("convwP", [128, 96])
    convbP = D("convbP", [128, 24])
    alogP = D("alogP", [16, 1])
    dtbP = D("dtbP", [16, 1])
    wout = D("wout", [NZ, E])
    wguT = D("wguT", [E, HFFN])
    wprojT = D("wprojT", [HALF_H, E])
    convw = D("convw", [128, 256])
    convb = D("convb", [128, 64])
    alog = D("alog", [32, 1])
    dtb = D("dtb", [32, 1])
    normw = D("normw", [128, 1])
    consts = D("consts", [128, 1024])
    ctrlf = D("ctrlf", [128, 8])
    ctrli = D("ctrli", [1, 16], I32)
    out_h = nc.dram_tensor("o", [E, SPAN], F32, kind="ExternalOutput").ap()
    dbg_h = nc.dram_tensor("dbg", [128, 20 * 512], F32, kind="ExternalOutput").ap() if dbg else None

    xn_scr = nc.dram_tensor("xn_scr", [E, SPAN], BF16).ap()
    qkv_scr = nc.dram_tensor("qkv_scr", [NQKV, SPAN], BF16).ap()
    z_scr = nc.dram_tensor("z_scr", [NZ, SPAN], BF16).ap()
    core_scr = nc.dram_tensor("core_scr", [NZ, SPAN], BF16).ap()
    h_scr = nc.dram_tensor("h_scr", [HALF_H, SPAN], BF16).ap()
    g_scr = nc.dram_tensor("g_scr", [E, SPAN], BF16).ap()
    wpq_scr = nc.dram_tensor("wpq_scr", [HALF_H, E], BF16).ap()
    gstat_scr = nc.dram_tensor("gstat_scr", [64, SPAN], F32).ap()
    kvP_scr = nc.dram_tensor("kvP_scr", [3072, 4096], BF16).ap()
    gstatP_scr = nc.dram_tensor("gstatP_scr", [32, 4096], F32).ap()
    stash_vi = nc.dram_tensor("stash_vi", [NSUB * KH, 64 * NCS * 2 * 128], BF16).ap()
    stash_kw = nc.dram_tensor("stash_kw", [NSUB * KH, 64 * NCS * 2 * 128], BF16).ap()
    stash_kc = nc.dram_tensor("stash_kc", [NSUB * KH, 128 * NCS * 2 * 64], BF16).ap()
    stash_at = nc.dram_tensor("stash_at", [NSUB * KH, 64 * NCS * 2 * 64], BF16).ap()
    stash_qg = nc.dram_tensor("stash_qg", [NSUB * KH, 128 * NCS * 2 * 64], BF16).ap()
    mbox_s = nc.dram_tensor("mbox_s", [128, VH * 128], F32, addr_space="Shared").ap()
    mbox_f = nc.dram_tensor("mbox_f", [1, 8], I32, addr_space="Shared").ap()

    S_ = lambda n, sh, dt: es.enter_context(nc.sbuf_tensor(n, sh, dt))
    cons = S_("cons", [128, 1024], F32)
    identf = cons[:, 0:128]
    bigLS = cons[:, 128:192]
    e63c = cons[:, 192:193]
    onecol = cons[:, 193:194]
    epsffn_c = cons[:, 194:195]
    sel8 = cons[0:NCS, 208:208 + NCS * C]
    bdiagf = cons[:, 720:848]
    identb = S_("identb", [128, 128], BF16)
    bdiagb = S_("bdiagb", [128, 128], BF16)
    onesb = S_("onesb", [128, 128], BF16)
    ones1b = S_("ones1b", [1, 128], BF16)
    nones8 = S_("nones8", [NCS, 128], F32)
    epsln = S_("epsln", [128, 1], F32)
    epsl2 = S_("epsl2", [128, 1], F32)
    lnq = S_("lnq", [128, 1], F32)
    wba_sb = S_("wba_sb", [128, KE * 64], BF16)
    convw_sb = S_("convw_sb", [128, 256], F32)
    convb_sb = S_("convb_sb", [128, 64], F32)
    alog_sb = S_("alog_sb", [32, 1], F32)
    dtb_sb = S_("dtb_sb", [32, 1], F32)
    nea_sb = S_("nea_sb", [32, 1], F32)
    convwP_sb = S_("convwP_sb", [128, 96], F32)
    convbP_sb = S_("convbP_sb", [128, 24], F32)
    alogP_sb = S_("alogP_sb", [16, 1], F32)
    dtbP_sb = S_("dtbP_sb", [16, 1], F32)
    neaP_sb = S_("neaP_sb", [16, 1], F32)
    normw_sb = S_("normw_sb", [128, 1], F32)
    ctrl_sb = S_("ctrl_sb", [128, 8], F32)
    ctrli_sb = S_("ctrli_sb", [1, 16], I32)

    Ssb = S_("Ssb", [128, VH * 128], F32)
    Sbf = S_("Sbf", [128, VH * 128], BF16)
    flg = S_("flg", [1, 8], I32)
    lamS = S_("lamS", [128, VH * NSC], F32)

    A1 = S_("arena1", [128, KE * (SPAN + 3)], BF16)
    A2F = S_("arena2f", [128, 6400], F32)
    A3 = S_("arena3", [128, 15360], BF16)
    b_sp = A2F[0:32, 0:2048]
    a_sp = A2F[0:32, 2048:4096]
    gtmp = A2F[0:32, 4096:6144]
    stateio = A2F[:, 2048:6144]
    decT_t = S_("decT_t", [64, 2 * NCS * C], BF16)

    xf = S_("xf", [128, SUB + 3], F32)
    sqv = S_("sqv", [128, SUB + 3], BF16)
    xbv = S_("xbv", [128, SUB + 3], BF16)
    meanb = S_("meanb", [128, SUB + 3], F32)
    varb = S_("varb", [128, SUB + 3], F32)
    m2b = S_("m2b", [128, SUB + 3], F32)

    gch = S_("gch", [NCS, VH * C], F32)
    gchr = S_("gchr", [NCS, VH * C], F32)
    bch = S_("bch", [NCS, VH * C], BF16)
    kwch = S_("kwch", [NCS, VH * C], BF16)
    zeros_ch = S_("zeros_ch", [NCS, C], F32)
    dp = S_("dp", [NCS, NCS * C], F32)
    bT2 = S_("bT2", [128, KH * NCS], F32)
    nbT2 = S_("nbT2", [128, KH * NCS], F32)
    bgT2 = S_("bgT2", [128, KH * NCS], F32)
    kwT2 = S_("kwT2", [128, KH * NCS], F32)
    grT2 = S_("grT2", [128, KH * NCS], F32)
    lrow = S_("lrow", [1, VH * NCS], BF16)

    Ablk = S_("Ablk", [128, 128], BF16)
    AblkT = S_("AblkT", [128, 128], BF16)
    Bv0 = S_("Bv0", [128, 128], BF16)
    Bv1 = S_("Bv1", [128, 128], BF16)
    BvT0 = S_("BvT0", [128, 128], BF16)
    BvT1 = S_("BvT1", [128, 128], BF16)
    Ptl = S_("Ptl", [128, 128], BF16)
    Rblk = S_("Rblk", [128, 256], BF16)
    kcdT_sb = S_("kcdT_sb", [64, 128], BF16)
    vnew_sb = S_("vnew_sb", [64, 128], BF16)
    Dmisc = S_("Dmisc", [128, VH * C], BF16)
    corehold = Dmisc[:, :]
    rstdrow = Dmisc[0:1, 0:SPAN]
    onesrow_b = S_("onesrow_b", [128, 1], BF16)

    P_ = lambda n, sh, dt: es.enter_context(nc.psum_tensor(n, sh, dt))
    psQ = [P_(f"psQ{i}", [128, 512], F32) for i in range(4)]
    psT = P_("psT", [128, 512], F32)
    psX = P_("psX", [128, 512], F32)
    psW = P_("psW", [128, 512], F32)
    psB16 = P_("psB16", [128, 1024], BF16)

    rec = Rec()
    SY, GP, VE, SC, TE = (Eng(rec, n) for n in ("sync", "gpsimd", "vector", "scalar", "tensor"))
    names = "ld x amm ac w w2 bmm bev bst d dc dmm dst sl smm sc f fmm fc e emm ec out ch po pp dg lg".split()
    ct = {n: Ctr(es, nc, n) for n in names}
    SELF.clear()
    LASTC.clear()
    SELF["ctr"] = Ctr(es, nc, "zz")
    (c_ld, c_x, c_amm, c_ac, c_w, c_w2, c_bmm, c_bev, c_bst, c_d, c_dc, c_dmm, c_dst,
     c_sl, c_smm, c_sc, c_f, c_fmm, c_fc, c_e, c_emm, c_ec, c_out, c_ch, c_po,
     c_pp, c_dg, c_lg) = (ct[n] for n in names)
    cwp = lambda i: c_w if i % 2 == 0 else c_w2

    def wt(eng, ctr, n=None):
        if n is None:
            for e, s in ctr.sub.items():
                rec.items[eng._name].append(("wait", s[0], s[1]))
        elif isinstance(n, dict):
            for e, v in n.items():
                rec.items[eng._name].append(("wait", ctr.sub[e][0], v))
        else:
            subs = list(ctr.sub.values())
            assert len(subs) <= 1, (ctr.name, list(ctr.sub))
            if subs:
                rec.items[eng._name].append(("wait", subs[0][0], n))

    def wtreg(eng, ctr, ap, src_eng="sync"):
        rec.items[eng._name].append(("waitreg", ctr.sem_for(src_eng), ap))

    def dbg_dump(slot, ap, ctr):
        if dbg_h is None:
            return
        wt(GP, ctr)
        c_out.dma(GP.dma_start(out=dbg_h[0:ap.shape[0], slot * 512:slot * 512 + ap.shape[1]], in_=ap))

    c_ld.dma(SY.dma_start(out=cons[:, :], in_=consts[:, :]))
    c_ld.dma(SY.dma_start(out=convw_sb[:], in_=convw[:, :]))
    c_ld.dma(SY.dma_start(out=convb_sb[:], in_=convb[:, :]))
    c_ld.dma(SY.dma_start(out=alog_sb[:], in_=alog[:, :]))
    c_ld.dma(SY.dma_start(out=dtb_sb[:], in_=dtb[:, :]))
    c_ld.dma(SY.dma_start(out=normw_sb[:], in_=normw[:, :]))
    c_ld.dma(SY.dma_start(out=ctrl_sb[:], in_=ctrlf[:, :]))
    c_ld.dma(SY.dma_start(out=ctrli_sb[:], in_=ctrli[:, :]))
    c_lg.dma(GP.dma_start(out=wba_sb[:, :].rearrange("p (k c) -> p k c", k=KE), in_=wbaP[:, :].rearrange("(k p) c -> p k c", p=128)))
    c_lg.dma(GP.dma_start(out=identb[:], in_=consts[:, 0:128]))
    c_lg.dma(GP.dma_start(out=bdiagb[:], in_=consts[:, 720:848]))
    c_ld.dma(SY.dma_start(out=convwP_sb[:], in_=convwP[:, :]))
    c_ld.dma(SY.dma_start(out=convbP_sb[:], in_=convbP[:, :]))
    c_ld.dma(SY.dma_start(out=alogP_sb[:], in_=alogP[:, :]))
    c_ld.dma(SY.dma_start(out=dtbP_sb[:], in_=dtbP[:, :]))
    wt(VE, c_ld)
    wt(VE, c_lg)
    VE.memset(onesb[:], 1.0)
    VE.memset(ones1b[:], 1.0)
    VE.memset(nones8[:], -1.0)
    VE.memset(epsln[:], 1e-5)
    VE.memset(epsl2[:], 1e-6)
    VE.memset(lnq[:], LN_DKI)
    VE.memset(zeros_ch[:], 0.0)
    VE.memset(Ssb[:], 0.0)
    VE.memset(Sbf[:], 0.0)
    VE.memset(stateio[:], 0.0)
    VE.memset(onesrow_b[:], 1.0)
    c_ac.inc(VE.memset(flg[:], 0))
    wt(SC, c_ld)
    wt(SC, c_lg)
    SC.activation(out=nea_sb[:], in_=alog_sb[:], func=AF.Exp)
    c_ac.inc(SC.mul(nea_sb[:], nea_sb[:], -1.0))
    SC.activation(out=neaP_sb[:], in_=alogP_sb[:], func=AF.Exp)
    c_ac.inc(SC.mul(neaP_sb[:], neaP_sb[:], -1.0))

    xnspP = lambda k: A1[:, k * (SPAN + 3):(k + 1) * (SPAN + 3)]
    qkP = lambda slot: A1[:, slot * SUB:(slot + 1) * SUB]
    wqP = [A3[:, i * 2048:(i + 1) * 2048] for i in range(2)]
    siltileP = A3[:, 4096:6144]
    rawtP = A2F[:, 0:2051]
    cv0P = A2F[:, 2051:4099]
    NTP = [(0, 512), (512, 512), (1024, 512), (1536, 512), (2048, 3)]
    bmm_histP = []
    for half in range(2):
        hb = half * 2048
        for st in range(NSUB):
            t0 = st * SUB
            x0 = 0 if st == 0 else 3 + t0
            cols = SUB + 3 if st == 0 else SUB
            parts = [(0, 512)] + ([(512, cols - 512)] if cols > 512 else [])
            for k in range(KE):
                wt(SY, c_ac)
                c_x.dma(SY.dma_start(out=xf[:, 0:cols], in_=xPT[k * 128:(k + 1) * 128, hb + x0:hb + x0 + cols]))
                wt(SC, c_x)
                wt(SC, c_amm)
                c_ac.inc(SC.activation(out=sqv[:, 0:cols], in_=xf[:, 0:cols], func=AF.Square))
                wt(VE, c_x)
                wt(VE, c_amm)
                if half == 1 and st == 0 and k == 0:
                    wt(VE, c_bmm)
                VE.tensor_copy(out=xbv[:, 0:cols], in_=xf[:, 0:cols])
                c_ac.inc(VE.tensor_copy(out=xnspP(k)[:, x0:x0 + cols], in_=xf[:, 0:cols]))
                wt(TE, c_ac)
                if half == 1 and st == 0 and k == 0:
                    wt(TE, c_bev)
                for pi, (o, w) in enumerate(parts):
                    pa = psX[:, 0:w] if pi == 0 else psT[:, 0:w]
                    pb = psW[:, 0:w] if pi == 0 else psT[:, 128:128 + w]
                    c_amm.inc(TE.matmul(pa, onesb[:, 0:128], xbv[:, o:o + w],
                                        start=(k == 0), stop=(k == KE - 1 and pi == 0)))
                    c_amm.inc(TE.matmul(pb, onesb[:, 0:128], sqv[:, o:o + w],
                                        start=(k == 0 and pi == 0), stop=(k == KE - 1)))
            wt(VE, c_amm)
            for pi, (o, w) in enumerate(parts):
                pa = psX[:, 0:w] if pi == 0 else psT[:, 0:w]
                pb = psW[:, 0:w] if pi == 0 else psT[:, 128:128 + w]
                VE.tensor_scalar(out=meanb[:, o:o + w], in0=pa, scalar1=1.0 / E, scalar2=None, op0=ALU.mult)
                VE.tensor_tensor(out=m2b[:, o:o + w], in0=meanb[:, o:o + w], in1=meanb[:, o:o + w], op=ALU.mult)
                c_ac.inc(VE.scalar_tensor_tensor(out=varb[:, o:o + w], in0=pb, scalar=1.0 / E,
                                                 in1=m2b[:, o:o + w], op0=ALU.mult, op1=ALU.subtract))
            wt(SC, c_ac)
            SC.activation(out=varb[:, 0:cols], in_=varb[:, 0:cols], func=AF.Ln, bias=epsln[:, 0:1])
            c_ac.inc(SC.activation(out=varb[:, 0:cols], in_=varb[:, 0:cols], func=AF.Exp, scale=-0.5))
            for k in range(KE):
                sl = xnspP(k)[:, x0:x0 + cols]
                wt(VE, c_ac)
                VE.tensor_tensor(out=sl, in0=sl, in1=meanb[:, 0:cols], op=ALU.subtract)
                c_ac.inc(VE.tensor_tensor(out=sl, in0=sl, in1=varb[:, 0:cols], op=ALU.mult))
            for k in range(KE):
                wt(TE, c_ac)
                c_amm.inc(TE.matmul(psX[0:64, 0:SUB], wba_sb[:, k * 64:(k + 1) * 64],
                                    xnspP(k)[:, 3 + t0:3 + t0 + SUB],
                                    start=(k == 0), stop=(k == KE - 1)))
            wt(SC, c_amm)
            wt(SC, c_bst)
            wt(SC, c_bev)
            SC.activation(out=b_sp[:, t0:t0 + SUB], in_=psX[0:32, 0:SUB], func=AF.Copy)
            c_ac.inc(SC.activation(out=a_sp[:, t0:t0 + SUB], in_=psX[32:64, 0:SUB], func=AF.Copy))
        wt(VE, c_ac)
        c_ac.inc(VE.tensor_scalar(out=gtmp[0:16, :], in0=a_sp[0:16, :], scalar1=dtbP_sb[0:16, 0:1],
                                  scalar2=None, op0=ALU.add))
        wt(SC, c_ac)
        c_ac.inc(SC.activation(out=b_sp[0:16, :], in_=b_sp[0:16, :], func=AF.Sigmoid))
        wt(SC, c_ac)
        c_ac.inc(SC.activation(out=a_sp[0:16, :], in_=gtmp[0:16, :], func=AF.Abs))
        SC.activation(out=a_sp[0:16, :], in_=a_sp[0:16, :], func=AF.Exp, scale=-1.0)
        c_ac.inc(SC.activation(out=a_sp[0:16, :], in_=a_sp[0:16, :], func=AF.Ln, bias=onecol[0:16, 0:1]))
        wt(VE, c_ac)
        VE.tensor_scalar(out=gtmp[0:16, :], in0=gtmp[0:16, :], scalar1=0.0, scalar2=None, op0=ALU.max)
        VE.tensor_tensor(out=a_sp[0:16, :], in0=a_sp[0:16, :], in1=gtmp[0:16, :], op=ALU.add)
        c_ac.inc(VE.tensor_scalar(out=a_sp[0:16, :], in0=a_sp[0:16, :],
                                  scalar1=neaP_sb[0:16, 0:1], scalar2=None, op0=ALU.mult))
        wt(SY, c_ac)
        c_bst.dma(SY.dma_start(out=gstatP_scr[0:16, hb:hb + 2048], in_=a_sp[0:16, :]))
        c_bst.dma(SY.dma_start(out=gstatP_scr[16:32, hb:hb + 2048], in_=b_sp[0:16, :]))
        for oc in range(24):
            buf = wqP[oc % 2]
            if len(bmm_histP) >= 2:
                wt(GP, c_bmm, bmm_histP[-2])
            cwp(oc).dma(GP.dma_start(out=buf[:, :].rearrange("p (k c) -> p k c", k=KE),
                                     in_=wkvP[:, oc * 128:(oc + 1) * 128].rearrange("(k p) c -> p k c", p=128)))
            wt(TE, cwp(oc))
            wt(TE, c_bev)
            for (o, w) in NTP:
                for k in range(KE):
                    ps = psQ[o // 512][:, 0:512] if w == 512 else psT[:, 0:3]
                    c_bmm.inc(TE.matmul(ps, buf[:, k * 128:(k + 1) * 128], xnspP(k)[:, o:o + w],
                                        start=(k == 0), stop=(k == KE - 1)))
            bmm_histP.append(c_bmm.n)
            wt(SC, c_bmm)
            wt(SC, c_bev)
            wt(SC, c_bst)
            for (o, w) in NTP:
                ps = psQ[o // 512][:, 0:w] if w == 512 else psT[:, 0:3]
                c_bev.inc(SC.activation(out=rawtP[:, o:o + w], in_=ps, func=AF.Copy))
            wt(VE, c_bev)
            VE.tensor_scalar(out=cv0P[:, 0:2048], in0=rawtP[:, 0:2048],
                             scalar1=convwP_sb[:, oc * 4:oc * 4 + 1], scalar2=None, op0=ALU.mult)
            VE.scalar_tensor_tensor(out=cv0P[:, 0:2048], in0=rawtP[:, 1:2049],
                                    scalar=convwP_sb[:, oc * 4 + 1:oc * 4 + 2],
                                    in1=cv0P[:, 0:2048], op0=ALU.mult, op1=ALU.add)
            VE.scalar_tensor_tensor(out=cv0P[:, 0:2048], in0=rawtP[:, 2:2050],
                                    scalar=convwP_sb[:, oc * 4 + 2:oc * 4 + 3],
                                    in1=cv0P[:, 0:2048], op0=ALU.mult, op1=ALU.add)
            c_bev.inc(VE.scalar_tensor_tensor(out=cv0P[:, 0:2048], in0=rawtP[:, 3:2051],
                                              scalar=convwP_sb[:, oc * 4 + 3:oc * 4 + 4],
                                              in1=cv0P[:, 0:2048], op0=ALU.mult, op1=ALU.add))
            wt(SC, c_bev)
            c_bev.inc(SC.activation(out=siltileP[:, 0:2048], in_=cv0P[:, 0:2048], func=AF.Silu,
                                    bias=convbP_sb[:, oc:oc + 1]))
            wt(SY, c_bev)
            c_bst.dma(SY.dma_start(out=kvP_scr[oc * 128:(oc + 1) * 128, hb:hb + 2048],
                                   in_=siltileP[:, 0:2048]))
    wt(GP, c_amm)
    c_lg.dma(GP.dma_start(out=wba_sb[:, :].rearrange("p (k c) -> p k c", k=KE),
                          in_=wba[:, :].rearrange("(k p) c -> p k c", p=128)))
    wt(VE, c_bst)
    wt(VE, c_bev)
    c_sc.inc(VE.memset(A2F[:, 0:2048], 0.0))

    decAP = A3[:, 0:512]
    vi_pairP = A3[:, 512:2560]
    kw_pairP = A3[:, 2560:4608]
    kc_pairP = A3[:, 4608:5632]
    for st in range(8):
        t0 = st * SUB
        wt(SY, c_dst)
        wt(SY, c_bst)
        wt(SY, c_bmm)
        wt(SY, c_dc)
        c_d.dma(SY.dma_start(out=A1[:, 16 * SUB:24 * SUB].rearrange("p (s t) -> p s t", s=8),
                             in_=kvP_scr[0:1024, t0:t0 + SUB].rearrange("(s p) t -> p s t", p=128)))
        c_d.dma(SY.dma_start(out=A1[:, 32 * SUB:48 * SUB].rearrange("p (s t) -> p s t", s=16),
                             in_=kvP_scr[1024:3072, t0:t0 + SUB].rearrange("(s p) t -> p s t", p=128)))
        for slot in range(16, 24):
            tile = qkP(slot)
            wt(SC, c_d)
            wt(SC, c_dc)
            wt(SC, c_dmm)
            wt(SC, c_amm)
            wt(SC, c_ac)
            c_dc.inc(SC.activation(out=sqv[:, 0:SUB], in_=tile, func=AF.Square))
            wt(TE, c_dc)
            c_dmm.inc(TE.matmul(psX[:, 0:SUB], onesb[:, 0:128], sqv[:, 0:SUB], start=True, stop=True))
            wt(SC, c_dmm)
            SC.activation(out=varb[:, 0:SUB], in_=psX[:, 0:SUB], func=AF.Ln, bias=epsl2[:, 0:1])
            c_dc.inc(SC.activation(out=varb[:, 0:SUB], in_=varb[:, 0:SUB], func=AF.Exp, scale=-0.5))
            wt(VE, c_dc)
            c_dc.inc(VE.tensor_tensor(out=tile, in0=tile, in1=varb[:, 0:SUB], op=ALU.mult))
        wt(GP, c_bst)
        wt(GP, c_dc)
        c_dg.dma(GP.dma_start(out=gchr[:, 0:1024].rearrange("c (h t) -> c h t", h=16),
                              in_=gstatP_scr[0:16, t0:t0 + SUB].rearrange("h (c t) -> c h t", c=NCS)))
        c_dg.dma(GP.dma_start(out=bch[:, 0:1024].rearrange("c (h t) -> c h t", h=16),
                              in_=gstatP_scr[16:32, t0:t0 + SUB].rearrange("h (c t) -> c h t", c=NCS)))
        wt(VE, c_d)
        wt(VE, c_dg)
        for h in range(16):
            c_dc.inc(VE.tensor_tensor_scan(out=gch[:, h * C:(h + 1) * C], data0=gchr[:, h * C:(h + 1) * C],
                                           data1=zeros_ch[:, 0:C], initial=0.0, op0=ALU.add, op1=ALU.add))
        wt(SC, c_dc)
        for h in range(16):
            c_dc.inc(SC.activation(out=kwch[:, h * C:(h + 1) * C], in_=gchr[:, h * C:(h + 1) * C],
                                   func=AF.Exp, bias=gchr[:, h * C + C - 1:h * C + C], scale=-1.0))
        for (dstt, srct, fp32) in ((bT2, bch, False), (kwT2, kwch, False), (grT2, gchr, True)):
            for h in range(16):
                wt(TE, c_dc)
                wt(TE, c_dmm)
                if fp32:
                    c_dmm.inc(TE.transpose(psX[0:C, 0:NCS], srct[:, h * C:(h + 1) * C], identf[0:NCS, 0:NCS]))
                else:
                    c_dmm.inc(TE.transpose(psB16[0:C, 0:NCS], srct[:, h * C:(h + 1) * C], identb[0:NCS, 0:NCS]))
                wt(VE, c_dmm)
                c_dc.inc(VE.tensor_copy(out=dstt[(h % 2) * 64:(h % 2) * 64 + 64,
                                                 (h // 2) * NCS:(h // 2 + 1) * NCS],
                                        in_=(psX[0:C, 0:NCS] if fp32 else psB16[0:C, 0:NCS])))
        wt(SC, c_dc)
        c_dc.inc(SC.activation(out=grT2[:, 0:64], in_=grT2[:, 0:64], func=AF.Exp))
        wt(VE, c_dc)
        VE.tensor_tensor(out=bgT2[:, 0:64], in0=bT2[:, 0:64], in1=grT2[:, 0:64], op=ALU.mult)
        c_dc.inc(VE.tensor_scalar(out=nbT2[:, 0:64], in0=bT2[:, 0:64], scalar1=-1.0, scalar2=None, op0=ALU.mult))
        wt(TE, c_dc)
        c_dmm.inc(TE.matmul(psX[0:1, 0:64], e63c[0:64, 0:1], grT2[0:64, 0:64], start=True, stop=True))
        c_dmm.inc(TE.matmul(psW[0:1, 0:64], e63c[64:128, 0:1], grT2[64:128, 0:64], start=True, stop=True))
        wt(VE, c_dmm)
        VE.tensor_copy(out=lrow[0:1, 0:128].rearrange("o (h c) -> o h c", c=NCS)[:, 0::2, :],
                       in_=psX[0:1, 0:64].rearrange("o (h c) -> o h c", c=NCS))
        c_dc.inc(VE.tensor_copy(out=lrow[0:1, 0:128].rearrange("o (h c) -> o h c", c=NCS)[:, 1::2, :],
                                in_=psW[0:1, 0:64].rearrange("o (h c) -> o h c", c=NCS)))
        wt(TE, c_dc)
        c_dmm.inc(TE.matmul(psW[:, 0:128], ones1b[:, 0:128], lrow[0:1, 0:128], start=True, stop=True))
        wt(VE, c_dmm)
        c_dc.inc(VE.tensor_copy(
            out=lamS[:, :].rearrange("p (h s c) -> p h s c", s=8, c=NCS)[:, :, st, :],
            in_=psW[:, 0:128].rearrange("p (h c) -> p h c", c=NCS)))

        for hp in range(8):
            wt(VE, c_dst)
            h0, h1 = 2 * hp, 2 * hp + 1
            for hh, h in ((0, h0), (1, h1)):
                tp = None if hh == 0 else (0, 64)
                wt(VE, c_dmm)
                c_dc.inc(VE.tensor_tensor(out=dp[:, :].rearrange("p (r t) -> p r t", r=NCS),
                                          in0=gch[:, h * C:(h + 1) * C].unsqueeze(1).broadcast_to([NCS, NCS, C]),
                                          in1=sel8[:, :].rearrange("p (r t) -> p r t", r=NCS),
                                          op=ALU.mult))
                wt(TE, c_dc)
                wt(TE, c_bev)
                c_dmm.inc(TE.matmul(psT[hh * 64:hh * 64 + 64, 0:NCS * C], gch[:, h * C:(h + 1) * C],
                                    sel8[:, :], start=True, stop=False, tile_position=tp))
                c_dmm.inc(TE.matmul(psT[hh * 64:hh * 64 + 64, 0:NCS * C], nones8[:, 0:64],
                                    dp[:, :], start=False, stop=True, tile_position=tp))
            wt(VE, c_dmm)
            c_dc.inc(VE.tensor_tensor(out=psT[:, 0:NCS * C].rearrange("p (r t) -> p r t", r=NCS),
                                      in0=psT[:, 0:NCS * C].rearrange("p (r t) -> p r t", r=NCS),
                                      in1=bigLS[:, :].unsqueeze(1).broadcast_to([128, NCS, C]),
                                      op=ALU.add))
            wt(VE, c_dc)
            c_dc.inc(VE.tensor_scalar(out=psT[:, 0:NCS * C], in0=psT[:, 0:NCS * C],
                                      scalar1=-30000.0, scalar2=None, op0=ALU.add))
            wt(SC, c_dc)
            c_dc.inc(SC.activation(out=decAP[:, :], in_=psT[:, 0:NCS * C], func=AF.Exp))

            for cch in range(NCS):
                col = hp * NCS + cch
                gcol = lambda t2, hh: t2[hh * 64:hh * 64 + 64, col:col + 1]
                kn = qkP(16 + hp)[:, cch * C:(cch + 1) * C]
                wt(TE, c_dc)
                c_dmm.inc(TE.matmul(psX[0:64, 0:64], kn, kn, start=True, stop=True))
                wt(VE, c_dmm)
                for hh in range(2):
                    c_dc.inc(VE.scalar_tensor_tensor(
                        out=Ablk[hh * 64:hh * 64 + 64, hh * 64:hh * 64 + 64],
                        in0=psX[0:64, 0:64], scalar=gcol(nbT2, hh),
                        in1=decAP[hh * 64:hh * 64 + 64, cch * C:(cch + 1) * C],
                        op0=ALU.mult, op1=ALU.mult))
                if st == 0 and hp == 0 and cch == 0:
                    VE.memset(Ablk[0:64, 64:128], 0.0)
                    c_dc.inc(VE.memset(Ablk[64:128, 0:64], 0.0))
                wt(TE, c_dc)
                c_dmm.inc(TE.transpose(psB16[:, 0:128], Ablk[:, :], identb[:, 0:128]))
                wt(VE, c_dmm)
                VE.tensor_copy(out=AblkT[:], in_=psB16[:, 0:128])
                VE.tensor_add(out=Ptl[:], in0=AblkT[:], in1=identb[:, 0:128])
                VE.tensor_copy(out=Bv0[:], in_=Ablk[:])
                c_dc.inc(VE.tensor_copy(out=BvT0[:], in_=AblkT[:]))
                for j in range(1, 6):
                    wt(TE, c_dc)
                    c_dmm.inc(TE.matmul(psW[:, 0:128], BvT0[:], Bv0[:], start=True, stop=True))
                    if j < 5:
                        c_dmm.inc(TE.matmul(psW[:, 128:256], Bv0[:], BvT0[:], start=True, stop=True))
                    wt(VE, c_dmm)
                    c_dc.inc(VE.tensor_copy(out=Bv1[:], in_=psW[:, 0:128]))
                    if j < 5:
                        c_dc.inc(VE.tensor_copy(out=BvT1[:], in_=psW[:, 128:256]))
                    wt(TE, c_dc)
                    c_dmm.inc(TE.matmul(psW[:, 256:384], Bv1[:], Ptl[:], start=True, stop=True))
                    wt(VE, c_dmm)
                    VE.tensor_add(out=Ptl[:], in0=Ptl[:], in1=psW[:, 256:384])
                    VE.tensor_copy(out=Bv0[:], in_=Bv1[:])
                    if j < 5:
                        c_dc.inc(VE.tensor_copy(out=BvT0[:], in_=BvT1[:]))
                    else:
                        c_dc.inc(VE.tensor_copy(out=Bv0[:], in_=Bv1[:]))
                wt(TE, c_dc)
                c_dmm.inc(TE.transpose(psB16[0:64, 128:256], kn, identb[:, 0:128]))
                wt(VE, c_dmm)
                for hh in range(2):
                    VE.tensor_scalar(out=Rblk[hh * 64:hh * 64 + 64, 128:256], in0=psB16[0:64, 128:256],
                                     scalar1=gcol(bgT2, hh), scalar2=None, op0=ALU.mult)
                    c_dc.inc(VE.tensor_scalar(out=kw_pairP[0:64, (cch * 2 + hh) * 128:(cch * 2 + hh + 1) * 128],
                                              in0=psB16[0:64, 128:256],
                                              scalar1=gcol(kwT2, hh), scalar2=None, op0=ALU.mult))
                for hh, h in ((0, h0), (1, h1)):
                    vt = qkP(32 + h)[:, cch * C:(cch + 1) * C]
                    wt(TE, c_dc)
                    c_dmm.inc(TE.transpose(psB16[0:64, 256:384], vt, identb[:, 0:128]))
                    wt(VE, c_dmm)
                    c_dc.inc(VE.tensor_scalar(out=Rblk[hh * 64:hh * 64 + 64, 0:128], in0=psB16[0:64, 256:384],
                                              scalar1=gcol(bT2, hh), scalar2=None, op0=ALU.mult))
                wt(TE, c_dc)
                c_dmm.inc(TE.matmul(psQ[0][:, 0:256], Ptl[:], Rblk[:], start=True, stop=True))
                wt(VE, c_dmm)
                for hh in range(2):
                    c_dc.inc(VE.tensor_copy(out=vi_pairP[0:64, (cch * 2 + hh) * 128:(cch * 2 + hh + 1) * 128],
                                            in_=psQ[0][hh * 64:hh * 64 + 64, 0:128]))
                    c_dc.inc(VE.tensor_copy(out=kcdT_sb[:, :], in_=psQ[0][hh * 64:hh * 64 + 64, 128:256]))
                    wt(TE, c_dc)
                    c_dmm.inc(TE.transpose(psB16[:, 384:448], kcdT_sb[:, :], identb[0:64, 0:64]))
                    wt(VE, c_dmm)
                    c_dc.inc(VE.tensor_scalar(out=kc_pairP[:, (cch * 2 + hh) * 64:(cch * 2 + hh + 1) * 64],
                                              in0=psB16[:, 384:448], scalar1=-1.0, scalar2=None, op0=ALU.mult))
            rowp = st * 8 + hp
            wt(SY, c_dc)
            c_dst.dma(SY.dma_start(out=stash_vi[rowp:rowp + 1, :].rearrange("o (p t) -> (o p) t", p=64),
                                   in_=vi_pairP[0:64, :]))
            c_dst.dma(SY.dma_start(out=stash_kw[rowp:rowp + 1, :].rearrange("o (p t) -> (o p) t", p=64),
                                   in_=kw_pairP[0:64, :]))
            c_dst.dma(SY.dma_start(out=stash_kc[rowp:rowp + 1, :].rearrange("o (p t) -> (o p) t", p=128),
                                   in_=kc_pairP[:, :]))
    prefP_dst = c_dst.n

    sc_viP = A3[:, 8192:10240]
    sc_kwP = A3[:, 0:2048]
    sc_kcP = A3[:, 4096:5120]
    for cg in range(64):
        stq, cch = cg // 8, cg % 8
        wt(GP, c_sc)
        wt(GP, c_dst, prefP_dst)
        r0 = stq * 8
        srcP = lambda t, p: t[r0:r0 + 8, :].rearrange("r (p c h t) -> p c r h t", p=p, c=NCS, h=2)[:, cch, :, :, :]
        c_sl.dma(GP.dma_start(out=sc_viP[0:64, :].rearrange("p (r h t) -> p r h t", r=8, h=2),
                              in_=srcP(stash_vi, 64)))
        c_sl.dma(GP.dma_start(out=sc_kwP[0:64, :].rearrange("p (r h t) -> p r h t", r=8, h=2),
                              in_=srcP(stash_kw, 64)))
        c_sl.dma(GP.dma_start(out=sc_kcP[:, :].rearrange("p (r h t) -> p r h t", r=8, h=2),
                              in_=srcP(stash_kc, 128)))
        vnew8P = A3[0:64, 14336:15360]
        for bb in range(2):
            hs = list(range(bb * 8, bb * 8 + 8))
            wt(TE, c_sl)
            wt(TE, c_sc)
            for i, h in enumerate(hs):
                pv = (psQ[2] if i < 4 else psQ[3])[0:64, (i % 4) * 128:(i % 4) * 128 + 128]
                c_smm.inc(TE.matmul(pv, identb[0:64, 0:64],
                                    sc_viP[0:64, h * 128:(h + 1) * 128],
                                    start=(i % 4 == 0), stop=False))
                c_smm.inc(TE.matmul(pv, sc_kcP[:, h * 64:(h + 1) * 64],
                                    Sbf[:, h * 128:(h + 1) * 128],
                                    start=False, stop=(i % 4 == 3)))
            wt(SC, c_smm)
            SC.activation(out=vnew8P[:, 0:512], in_=psQ[2][0:64, 0:512], func=AF.Copy)
            c_sc.inc(SC.activation(out=vnew8P[:, 512:1024], in_=psQ[3][0:64, 0:512], func=AF.Copy))
            wt(TE, c_sc)
            for i, h in enumerate(hs):
                vsl = vnew8P[0:64, i * 128:(i + 1) * 128]
                ps = (psT if i < 4 else psW)[:, (i % 4) * 128:(i % 4) * 128 + 128]
                c_smm.inc(TE.matmul(ps, sc_kwP[0:64, h * 128:(h + 1) * 128], vsl,
                                    start=(i % 4 == 0), stop=(i % 4 == 3)))
            wt(VE, c_smm)
            for i, h in enumerate(hs):
                ps = (psT if i < 4 else psW)[:, (i % 4) * 128:(i % 4) * 128 + 128]
                c_sc.inc(VE.scalar_tensor_tensor(out=A2F[:, h * 128:(h + 1) * 128],
                                                 in0=A2F[:, h * 128:(h + 1) * 128],
                                                 scalar=lamS[:, h * 64 + cg:h * 64 + cg + 1],
                                                 in1=ps, op0=ALU.mult, op1=ALU.add))
            wt(GP, c_sc)
            c_sc.inc(GP.tensor_copy(out=Sbf[:, bb * 1024:(bb + 1) * 1024],
                                    in_=A2F[:, bb * 1024:(bb + 1) * 1024]))
    prefP_sc = c_sc.snap()

    wt(GP, c_sc, prefP_sc)
    iffp = []
    rec.items["gpsimd"].append(("iff", ctrli_sb[0:1, 7:8], iffp))
    c_pp.dma(GP.dma_start(out=mbox_s[:, 0:2048], in_=A2F[:, 0:2048]))
    rec.items["gpsimd"].append(("wait", c_pp.sem_for("gpsimd"), c_pp.n_of("gpsimd")))
    c_pp.dma(GP.dma_start(out=mbox_f[0:1, 1:2], in_=ctrli[0:1, 4:5]))
    rec.items["gpsimd"].append(("wait", c_pp.sem_for("gpsimd"), c_pp.n_of("gpsimd")))
    rec.items["gpsimd"].append(("endif", iffp))
    wt(SY, c_bev)
    wt(SY, c_ac)
    iffr = []
    rec.items["sync"].append(("iff", ctrli_sb[0:1, 5:6], iffr))
    rec.items["sync"].append(("poll", c_pp.sem_for("sync"), flg[0:1, 1:2], mbox_f[0:1, 1:2], ctrli_sb[0:1, 4:5], 0))
    c_ch.dma(SY.dma_start(out=A2F[:, 2048:4096], in_=mbox_s[:, 0:2048]))
    rec.items["sync"].append(("endif", iffr))
    wt(VE, c_sc, prefP_sc)
    wtreg(VE, c_ch, ctrli_sb[0:1, 6:7])
    VE.tensor_scalar(out=Ssb[:, 0:2048], in0=A2F[:, 0:2048], scalar1=ctrl_sb[:, 1:2],
                     scalar2=None, op0=ALU.mult)
    c_sc.inc(VE.tensor_scalar(out=Ssb[:, 2048:4096], in0=A2F[:, 2048:4096], scalar1=ctrl_sb[:, 1:2],
                              scalar2=None, op0=ALU.mult))

    for beng in (SY, GP, VE, SC, TE):
        for bctr in (c_ld, c_lg, c_x, c_amm, c_ac, c_w, c_w2, c_bmm, c_bev, c_bst,
                     c_d, c_dg, c_dc, c_dmm, c_dst, c_sl, c_smm, c_sc):
            wt(beng, bctr)

    xnsp = lambda k: A1[:, k * (SPAN + 3):(k + 1) * (SPAN + 3)]
    for st in range(NSUB):
        t0 = st * SUB
        x0 = 0 if st == 0 else 3 + t0
        cols = SUB + 3 if st == 0 else SUB
        parts = [(0, 512)] + ([(512, cols - 512)] if cols > 512 else [])
        for k in range(KE):
            wt(SY, c_ac)
            c_x.dma(SY.dma_start(out=xf[:, 0:cols], in_=xT[k * 128:(k + 1) * 128, x0:x0 + cols]))
            wt(SC, c_x)
            wt(SC, c_amm)
            c_ac.inc(SC.activation(out=sqv[:, 0:cols], in_=xf[:, 0:cols], func=AF.Square))
            wt(VE, c_x)
            wt(VE, c_amm)
            VE.tensor_copy(out=xbv[:, 0:cols], in_=xf[:, 0:cols])
            c_ac.inc(VE.tensor_copy(out=xnsp(k)[:, x0:x0 + cols], in_=xf[:, 0:cols]))
            wt(TE, c_ac)
            for pi, (o, w) in enumerate(parts):
                pa = psX[:, 0:w] if pi == 0 else psT[:, 0:w]
                pb = psW[:, 0:w] if pi == 0 else psT[:, 128:128 + w]
                c_amm.inc(TE.matmul(pa, onesb[:, 0:128], xbv[:, o:o + w],
                                    start=(k == 0), stop=(k == KE - 1 and pi == 0)))
                c_amm.inc(TE.matmul(pb, onesb[:, 0:128], sqv[:, o:o + w],
                                    start=(k == 0 and pi == 0), stop=(k == KE - 1)))
        wt(VE, c_amm)
        for pi, (o, w) in enumerate(parts):
            pa = psX[:, 0:w] if pi == 0 else psT[:, 0:w]
            pb = psW[:, 0:w] if pi == 0 else psT[:, 128:128 + w]
            VE.tensor_scalar(out=meanb[:, o:o + w], in0=pa, scalar1=1.0 / E, scalar2=None, op0=ALU.mult)
            VE.tensor_tensor(out=m2b[:, o:o + w], in0=meanb[:, o:o + w], in1=meanb[:, o:o + w], op=ALU.mult)
            c_ac.inc(VE.scalar_tensor_tensor(out=varb[:, o:o + w], in0=pb, scalar=1.0 / E,
                                             in1=m2b[:, o:o + w], op0=ALU.mult, op1=ALU.subtract))
        wt(SC, c_ac)
        SC.activation(out=varb[:, 0:cols], in_=varb[:, 0:cols], func=AF.Ln, bias=epsln[:, 0:1])
        c_ac.inc(SC.activation(out=varb[:, 0:cols], in_=varb[:, 0:cols], func=AF.Exp, scale=-0.5))
        for k in range(KE):
            sl = xnsp(k)[:, x0:x0 + cols]
            wt(VE, c_ac)
            VE.tensor_tensor(out=sl, in0=sl, in1=meanb[:, 0:cols], op=ALU.subtract)
            c_ac.inc(VE.tensor_tensor(out=sl, in0=sl, in1=varb[:, 0:cols], op=ALU.mult))
            wt(SY, c_ac)
            c_bst.dma(SY.dma_start(out=xn_scr[k * 128:(k + 1) * 128, t0:t0 + SUB],
                                   in_=xnsp(k)[:, 3 + t0:3 + t0 + SUB]))
        for k in range(KE):
            wt(TE, c_ac)
            c_amm.inc(TE.matmul(psX[0:64, 0:SUB], wba_sb[:, k * 64:(k + 1) * 64],
                                xnsp(k)[:, 3 + t0:3 + t0 + SUB],
                                start=(k == 0), stop=(k == KE - 1)))
        wt(SC, c_amm)
        SC.activation(out=b_sp[:, t0:t0 + SUB], in_=psX[0:32, 0:SUB], func=AF.Copy)
        c_ac.inc(SC.activation(out=a_sp[:, t0:t0 + SUB], in_=psX[32:64, 0:SUB], func=AF.Copy))

    wt(VE, c_ac)
    c_ac.inc(VE.tensor_scalar(out=gtmp[:], in0=a_sp[:], scalar1=dtb_sb[0:32, 0:1],
                              scalar2=None, op0=ALU.add))
    wt(SC, c_ac)
    c_ac.inc(SC.activation(out=b_sp[:], in_=b_sp[:], func=AF.Sigmoid))
    wt(SC, c_ac)
    c_ac.inc(SC.activation(out=a_sp[:], in_=gtmp[:], func=AF.Abs))
    SC.activation(out=a_sp[:], in_=a_sp[:], func=AF.Exp, scale=-1.0)
    c_ac.inc(SC.activation(out=a_sp[:], in_=a_sp[:], func=AF.Ln, bias=onecol[0:32, 0:1]))
    wt(VE, c_ac)
    VE.tensor_scalar(out=gtmp[:], in0=gtmp[:], scalar1=0.0, scalar2=None, op0=ALU.max)
    VE.tensor_tensor(out=a_sp[:], in0=a_sp[:], in1=gtmp[:], op=ALU.add)
    c_ac.inc(VE.tensor_scalar(out=a_sp[:], in0=a_sp[:],
                              scalar1=nea_sb[0:32, 0:1], scalar2=None, op0=ALU.mult))
    wt(SY, c_ac)
    c_bst.dma(SY.dma_start(out=gstat_scr[0:32, :], in_=a_sp[:]))
    c_bst.dma(SY.dma_start(out=gstat_scr[32:64, :], in_=b_sp[:]))
    phA_bst = c_bst.n
    dbg_dump(0, xnsp(0)[:, 3:3 + 512], c_ac)
    dbg_dump(1, a_sp[:, 0:512], c_ac)
    dbg_dump(2, b_sp[:, 0:512], c_ac)

    wq = [A3[:, i * 2048:(i + 1) * 2048] for i in range(2)]
    siltile = A3[:, 4096:6144]
    rawt = A2F[:, 0:2051]
    cv0 = A2F[:, 2051:4099]
    NT = [(0, 512), (512, 512), (1024, 512), (1536, 512), (2048, 3)]
    bmm_hist = []
    for oc in range(96):
        buf = wq[oc % 2]
        if oc >= 2:
            wt(GP, c_bmm, bmm_hist[oc - 2])
        cwp(oc).dma(GP.dma_start(out=buf[:, :].rearrange("p (k c) -> p k c", k=KE),
                                 in_=wqkvz[:, oc * 128:(oc + 1) * 128].rearrange("(k p) c -> p k c", p=128)))
        wt(TE, cwp(oc))
        wt(TE, c_bev)
        for (o, w) in NT:
            for k in range(KE):
                ps = psQ[o // 512][:, 0:512] if w == 512 else psT[:, 0:3]
                c_bmm.inc(TE.matmul(ps, buf[:, k * 128:(k + 1) * 128], xnsp(k)[:, o:o + w],
                                    start=(k == 0), stop=(k == KE - 1)))
        bmm_hist.append(c_bmm.n)
        wt(SC, c_bmm)
        wt(SC, c_bev)
        wt(SC, c_bst)
        if oc < 64:
            for (o, w) in NT:
                ps = psQ[o // 512][:, 0:w] if w == 512 else psT[:, 0:3]
                c_bev.inc(SC.activation(out=rawt[:, o:o + w], in_=ps, func=AF.Copy))
            wt(VE, c_bev)
            VE.tensor_scalar(out=cv0[:, 0:2048], in0=rawt[:, 0:2048],
                             scalar1=convw_sb[:, oc * 4:oc * 4 + 1], scalar2=None, op0=ALU.mult)
            VE.scalar_tensor_tensor(out=cv0[:, 0:2048], in0=rawt[:, 1:2049],
                                    scalar=convw_sb[:, oc * 4 + 1:oc * 4 + 2],
                                    in1=cv0[:, 0:2048], op0=ALU.mult, op1=ALU.add)
            VE.scalar_tensor_tensor(out=cv0[:, 0:2048], in0=rawt[:, 2:2050],
                                    scalar=convw_sb[:, oc * 4 + 2:oc * 4 + 3],
                                    in1=cv0[:, 0:2048], op0=ALU.mult, op1=ALU.add)
            c_bev.inc(VE.scalar_tensor_tensor(out=cv0[:, 0:2048], in0=rawt[:, 3:2051],
                                              scalar=convw_sb[:, oc * 4 + 3:oc * 4 + 4],
                                              in1=cv0[:, 0:2048], op0=ALU.mult, op1=ALU.add))
            wt(SC, c_bev)
            c_bev.inc(SC.activation(out=siltile[:, 0:2048], in_=cv0[:, 0:2048], func=AF.Silu,
                                    bias=convb_sb[:, oc:oc + 1]))
            wt(SY, c_bev)
            c_bst.dma(SY.dma_start(out=qkv_scr[oc * 128:(oc + 1) * 128, :], in_=siltile[:, 0:2048]))
            if oc == 0:
                dbg_dump(3, rawt[:, 0:512], c_bev)
                dbg_dump(4, siltile[:, 0:512], c_bev)
        else:
            for (o, w) in NT:
                if o == 0:
                    c_bev.inc(SC.activation(out=siltile[:, 0:509], in_=psQ[0][:, 3:512], func=AF.Copy))
                elif w == 512:
                    c_bev.inc(SC.activation(out=siltile[:, o - 3:o + 509], in_=psQ[o // 512][:, 0:512], func=AF.Copy))
                else:
                    c_bev.inc(SC.activation(out=siltile[:, 2045:2048], in_=psT[:, 0:3], func=AF.Copy))
            wt(SY, c_bev)
            c_bst.dma(SY.dma_start(out=z_scr[(oc - 64) * 128:(oc - 63) * 128, :], in_=siltile[:, 0:2048]))
    phB_bst = c_bst.n

    qk = lambda slot: A1[:, slot * SUB:(slot + 1) * SUB]
    decA_p = A3[:, 0:512]
    vi_pair = A3[:, 512:2560]
    kw_pair = A3[:, 2560:4608]
    kc_pair = A3[:, 4608:5632]
    at_pair = A3[:, 5632:6656]
    qg_pair = A3[:, 6656:7680]
    qgb = A3[:, 7680:8192]
    for st in range(NSUB):
        t0 = st * SUB
        wt(SY, c_dst)
        wt(SY, c_bst, phB_bst)
        wt(SY, c_dc)
        c_d.dma(SY.dma_start(out=A1[:, 0:64 * SUB].rearrange("p (s t) -> p s t", s=64),
                             in_=qkv_scr[:, t0:t0 + SUB].rearrange("(s p) t -> p s t", p=128)))
        for sl in range(32):
            tile = qk(sl)
            wt(SC, c_d)
            wt(SC, c_dc)
            wt(SC, c_dmm)
            c_dc.inc(SC.activation(out=sqv[:, 0:SUB], in_=tile, func=AF.Square))
            wt(TE, c_dc)
            c_dmm.inc(TE.matmul(psX[:, 0:SUB], onesb[:, 0:128], sqv[:, 0:SUB], start=True, stop=True))
            wt(SC, c_dmm)
            SC.activation(out=varb[:, 0:SUB], in_=psX[:, 0:SUB], func=AF.Ln, bias=epsl2[:, 0:1])
            if sl < 16:
                c_dc.inc(SC.activation(out=varb[:, 0:SUB], in_=varb[:, 0:SUB], func=AF.Exp,
                                       scale=-0.5, bias=lnq[:, 0:1]))
            else:
                c_dc.inc(SC.activation(out=varb[:, 0:SUB], in_=varb[:, 0:SUB], func=AF.Exp, scale=-0.5))
            wt(VE, c_dc)
            c_dc.inc(VE.tensor_tensor(out=tile, in0=tile, in1=varb[:, 0:SUB], op=ALU.mult))
        wt(GP, c_bst, phA_bst)
        wt(GP, c_dc)
        c_dg.dma(GP.dma_start(out=gchr[:, :].rearrange("c (h t) -> c h t", h=VH),
                              in_=gstat_scr[0:32, t0:t0 + SUB].rearrange("h (c t) -> c h t", c=NCS)))
        c_dg.dma(GP.dma_start(out=bch[:, :].rearrange("c (h t) -> c h t", h=VH),
                              in_=gstat_scr[32:64, t0:t0 + SUB].rearrange("h (c t) -> c h t", c=NCS)))
        wt(VE, c_d)
        wt(VE, c_dg)
        for h in range(VH):
            c_dc.inc(VE.tensor_tensor_scan(out=gch[:, h * C:(h + 1) * C], data0=gchr[:, h * C:(h + 1) * C],
                                           data1=zeros_ch[:, 0:C], initial=0.0, op0=ALU.add, op1=ALU.add))
        wt(SC, c_dc)
        for h in range(VH):
            c_dc.inc(SC.activation(out=kwch[:, h * C:(h + 1) * C], in_=gchr[:, h * C:(h + 1) * C],
                                   func=AF.Exp, bias=gchr[:, h * C + C - 1:h * C + C], scale=-1.0))
        for (dst, srct, fp32) in ((bT2, bch, False), (kwT2, kwch, False), (grT2, gchr, True)):
            for h in range(VH):
                wt(TE, c_dc)
                wt(TE, c_dmm)
                if fp32:
                    c_dmm.inc(TE.transpose(psX[0:C, 0:NCS], srct[:, h * C:(h + 1) * C], identf[0:NCS, 0:NCS]))
                else:
                    c_dmm.inc(TE.transpose(psB16[0:C, 0:NCS], srct[:, h * C:(h + 1) * C], identb[0:NCS, 0:NCS]))
                wt(VE, c_dmm)
                c_dc.inc(VE.tensor_copy(out=dst[(h % 2) * 64:(h % 2) * 64 + 64,
                                                (h // 2) * NCS:(h // 2 + 1) * NCS],
                                        in_=(psX[0:C, 0:NCS] if fp32 else psB16[0:C, 0:NCS])))
        wt(SC, c_dc)
        c_dc.inc(SC.activation(out=grT2[:], in_=grT2[:], func=AF.Exp))
        wt(VE, c_dc)
        VE.tensor_tensor(out=bgT2[:], in0=bT2[:], in1=grT2[:], op=ALU.mult)
        c_dc.inc(VE.tensor_scalar(out=nbT2[:], in0=bT2[:], scalar1=-1.0, scalar2=None, op0=ALU.mult))
        wt(TE, c_dc)
        c_dmm.inc(TE.matmul(psX[0:1, 0:KH * NCS], e63c[0:64, 0:1], grT2[0:64, :], start=True, stop=True))
        c_dmm.inc(TE.matmul(psW[0:1, 0:KH * NCS], e63c[64:128, 0:1], grT2[64:128, :], start=True, stop=True))
        wt(VE, c_dmm)
        VE.tensor_copy(out=lrow[0:1, :].rearrange("o (h c) -> o h c", c=NCS)[:, 0::2, :],
                       in_=psX[0:1, 0:KH * NCS].rearrange("o (h c) -> o h c", c=NCS))
        c_dc.inc(VE.tensor_copy(out=lrow[0:1, :].rearrange("o (h c) -> o h c", c=NCS)[:, 1::2, :],
                                in_=psW[0:1, 0:KH * NCS].rearrange("o (h c) -> o h c", c=NCS)))
        wt(TE, c_dc)
        c_dmm.inc(TE.matmul(psW[:, 0:VH * NCS], ones1b[:, 0:128], lrow[:, :], start=True, stop=True))
        wt(VE, c_dmm)
        c_dc.inc(VE.tensor_copy(
            out=lamS[:, :].rearrange("p (h s c) -> p h s c", s=NSUB, c=NCS)[:, :, st, :],
            in_=psW[:, 0:VH * NCS].rearrange("p (h c) -> p h c", c=NCS)))

        for hp in range(KH):
            wt(VE, c_dst)
            h0, h1 = 2 * hp, 2 * hp + 1
            for hh, h in ((0, h0), (1, h1)):
                tp = None if hh == 0 else (0, 64)
                wt(VE, c_dmm)
                c_dc.inc(VE.tensor_tensor(out=dp[:, :].rearrange("p (r t) -> p r t", r=NCS),
                                          in0=gch[:, h * C:(h + 1) * C].unsqueeze(1).broadcast_to([NCS, NCS, C]),
                                          in1=sel8[:, :].rearrange("p (r t) -> p r t", r=NCS),
                                          op=ALU.mult))
                wt(TE, c_dc)
                wt(TE, c_bev)
                c_dmm.inc(TE.matmul(psT[hh * 64:hh * 64 + 64, 0:NCS * C], gch[:, h * C:(h + 1) * C],
                                    sel8[:, :], start=True, stop=False, tile_position=tp))
                c_dmm.inc(TE.matmul(psT[hh * 64:hh * 64 + 64, 0:NCS * C], nones8[:, 0:64],
                                    dp[:, :], start=False, stop=True, tile_position=tp))
            wt(VE, c_dmm)
            c_dc.inc(VE.tensor_tensor(out=psT[:, 0:NCS * C].rearrange("p (r t) -> p r t", r=NCS),
                                      in0=psT[:, 0:NCS * C].rearrange("p (r t) -> p r t", r=NCS),
                                      in1=bigLS[:, :].unsqueeze(1).broadcast_to([128, NCS, C]),
                                      op=ALU.add))
            wt(SC, c_dc)
            for hh in range(2):
                c_dc.inc(SC.activation(out=decT_t[:, hh * 512:(hh + 1) * 512],
                                       in_=psT[hh * 64:hh * 64 + 64, 0:NCS * C], func=AF.Exp, scale=-1.0))
            wt(VE, c_dc)
            c_dc.inc(VE.tensor_scalar(out=psT[:, 0:NCS * C], in0=psT[:, 0:NCS * C],
                                      scalar1=-30000.0, scalar2=None, op0=ALU.add))
            wt(SC, c_dc)
            c_dc.inc(SC.activation(out=decA_p[:, :], in_=psT[:, 0:NCS * C], func=AF.Exp))
            if st == 0 and hp == 15:
                dbg_dump(13, decT_t[:, 512:1024], c_dc)
                dbg_dump(14, gch[0:8, 30 * C:32 * C], c_dc)
                dbg_dump(9, decA_p[:, 0:512], c_dc)

            for hh, h in ((0, h0), (1, h1)):
                wt(VE, c_dc)
                c_dc.inc(VE.tensor_tensor(out=dp[:, :].rearrange("p (r t) -> p r t", r=NCS),
                                          in0=gchr[:, h * C:(h + 1) * C].unsqueeze(1).broadcast_to([NCS, NCS, C]),
                                          in1=sel8[:, :].rearrange("p (r t) -> p r t", r=NCS),
                                          op=ALU.mult))
                wt(SC, c_dc)
                c_dc.inc(SC.activation(out=dp[:, :], in_=dp[:, :], func=AF.Exp))
                wt(VE, c_dc)
                c_dc.inc(VE.tensor_tensor(out=dp[:, :], in0=dp[:, :], in1=sel8[:, :], op=ALU.mult))
                wt(TE, c_dc)
                c_dmm.inc(TE.matmul(psW[:, 0:NCS * C], nones8[:, 0:128], dp[:, :], start=True, stop=True))
                wt(VE, c_dmm)
                c_dc.inc(VE.scalar_tensor_tensor(out=qgb[:, :], in0=psW[:, 0:NCS * C], scalar=-1.0,
                                                 in1=qk(hp), op0=ALU.mult, op1=ALU.mult))
                for cch in range(NCS):
                    c_dc.inc(VE.tensor_copy(out=qg_pair[:, (cch * 2 + hh) * 64:(cch * 2 + hh + 1) * 64],
                                            in_=qgb[:, cch * C:(cch + 1) * C]))

            for cch in range(NCS):
                col = hp * NCS + cch
                gcol = lambda t2, hh: t2[hh * 64:hh * 64 + 64, col:col + 1]
                kn = qk(16 + hp)[:, cch * C:(cch + 1) * C]
                qn = qk(hp)[:, cch * C:(cch + 1) * C]
                wt(TE, c_dc)
                c_dmm.inc(TE.matmul(psX[0:64, 0:64], kn, kn, start=True, stop=True))
                wt(VE, c_dmm)
                for hh in range(2):
                    c_dc.inc(VE.scalar_tensor_tensor(
                        out=Ablk[hh * 64:hh * 64 + 64, hh * 64:hh * 64 + 64],
                        in0=psX[0:64, 0:64], scalar=gcol(nbT2, hh),
                        in1=decA_p[hh * 64:hh * 64 + 64, cch * C:(cch + 1) * C],
                        op0=ALU.mult, op1=ALU.mult))
                if st == 0 and hp == 0 and cch == 0:
                    VE.memset(Ablk[0:64, 64:128], 0.0)
                    c_dc.inc(VE.memset(Ablk[64:128, 0:64], 0.0))
                wt(TE, c_dc)
                c_dmm.inc(TE.transpose(psB16[:, 0:128], Ablk[:, :], identb[:, 0:128]))
                wt(VE, c_dmm)
                VE.tensor_copy(out=AblkT[:], in_=psB16[:, 0:128])
                VE.tensor_add(out=Ptl[:], in0=AblkT[:], in1=identb[:, 0:128])
                VE.tensor_copy(out=Bv0[:], in_=Ablk[:])
                c_dc.inc(VE.tensor_copy(out=BvT0[:], in_=AblkT[:]))
                for j in range(1, 6):
                    wt(TE, c_dc)
                    c_dmm.inc(TE.matmul(psW[:, 0:128], BvT0[:], Bv0[:], start=True, stop=True))
                    if j < 5:
                        c_dmm.inc(TE.matmul(psW[:, 128:256], Bv0[:], BvT0[:], start=True, stop=True))
                    wt(VE, c_dmm)
                    c_dc.inc(VE.tensor_copy(out=Bv1[:], in_=psW[:, 0:128]))
                    if j < 5:
                        c_dc.inc(VE.tensor_copy(out=BvT1[:], in_=psW[:, 128:256]))
                    wt(TE, c_dc)
                    c_dmm.inc(TE.matmul(psW[:, 256:384], Bv1[:], Ptl[:], start=True, stop=True))
                    wt(VE, c_dmm)
                    VE.tensor_add(out=Ptl[:], in0=Ptl[:], in1=psW[:, 256:384])
                    VE.tensor_copy(out=Bv0[:], in_=Bv1[:])
                    if j < 5:
                        c_dc.inc(VE.tensor_copy(out=BvT0[:], in_=BvT1[:]))
                    else:
                        c_dc.inc(VE.tensor_copy(out=Bv0[:], in_=Bv1[:]))
                wt(TE, c_dc)
                c_dmm.inc(TE.transpose(psB16[0:64, 128:256], kn, identb[:, 0:128]))
                wt(VE, c_dmm)
                for hh in range(2):
                    VE.tensor_scalar(out=Rblk[hh * 64:hh * 64 + 64, 128:256], in0=psB16[0:64, 128:256],
                                     scalar1=gcol(bgT2, hh), scalar2=None, op0=ALU.mult)
                    c_dc.inc(VE.tensor_scalar(out=kw_pair[0:64, (cch * 2 + hh) * 128:(cch * 2 + hh + 1) * 128],
                                              in0=psB16[0:64, 128:256],
                                              scalar1=gcol(kwT2, hh), scalar2=None, op0=ALU.mult))
                for hh, h in ((0, h0), (1, h1)):
                    vt = qk(32 + h)[:, cch * C:(cch + 1) * C]
                    wt(TE, c_dc)
                    c_dmm.inc(TE.transpose(psB16[0:64, 256:384], vt, identb[:, 0:128]))
                    wt(VE, c_dmm)
                    c_dc.inc(VE.tensor_scalar(out=Rblk[hh * 64:hh * 64 + 64, 0:128], in0=psB16[0:64, 256:384],
                                              scalar1=gcol(bT2, hh), scalar2=None, op0=ALU.mult))
                wt(TE, c_dc)
                c_dmm.inc(TE.matmul(psQ[0][:, 0:256], Ptl[:], Rblk[:], start=True, stop=True))
                wt(VE, c_dmm)
                for hh in range(2):
                    c_dc.inc(VE.tensor_copy(out=vi_pair[0:64, (cch * 2 + hh) * 128:(cch * 2 + hh + 1) * 128],
                                            in_=psQ[0][hh * 64:hh * 64 + 64, 0:128]))
                    c_dc.inc(VE.tensor_copy(out=kcdT_sb[:, :], in_=psQ[0][hh * 64:hh * 64 + 64, 128:256]))
                    wt(TE, c_dc)
                    c_dmm.inc(TE.transpose(psB16[:, 384:448], kcdT_sb[:, :], identb[0:64, 0:64]))
                    wt(VE, c_dmm)
                    c_dc.inc(VE.tensor_scalar(out=kc_pair[:, (cch * 2 + hh) * 64:(cch * 2 + hh + 1) * 64],
                                              in0=psB16[:, 384:448], scalar1=-1.0, scalar2=None, op0=ALU.mult))
                wt(TE, c_dc)
                c_dmm.inc(TE.matmul(psX[0:64, 64:128], kn, qn, start=True, stop=True))
                wt(VE, c_dmm)
                for hh in range(2):
                    c_dc.inc(VE.tensor_tensor(out=at_pair[0:64, (cch * 2 + hh) * 64:(cch * 2 + hh + 1) * 64],
                                              in0=psX[0:64, 64:128],
                                              in1=decT_t[:, hh * 512 + cch * C:hh * 512 + (cch + 1) * C],
                                              op=ALU.mult))
            if st == 0 and hp == 0:
                dbg_dump(5, grT2[:, 0:128], c_dc)
                dbg_dump(6, vi_pair[0:64, 0:512], c_dc)
                dbg_dump(7, kc_pair[:, 0:512], c_dc)
                dbg_dump(8, at_pair[0:64, 0:512], c_dc)
            row = st * KH + hp
            wt(SY, c_dc)
            c_dst.dma(SY.dma_start(out=stash_vi[row:row + 1, :].rearrange("o (p t) -> (o p) t", p=64),
                                   in_=vi_pair[0:64, :]))
            c_dst.dma(SY.dma_start(out=stash_kw[row:row + 1, :].rearrange("o (p t) -> (o p) t", p=64),
                                   in_=kw_pair[0:64, :]))
            c_dst.dma(SY.dma_start(out=stash_kc[row:row + 1, :].rearrange("o (p t) -> (o p) t", p=128),
                                   in_=kc_pair[:, :]))
            c_dst.dma(SY.dma_start(out=stash_at[row:row + 1, :].rearrange("o (p t) -> (o p) t", p=64),
                                   in_=at_pair[0:64, :]))
            c_dst.dma(SY.dma_start(out=stash_qg[row:row + 1, :].rearrange("o (p t) -> (o p) t", p=128),
                                   in_=qg_pair[:, :]))
    prep_dst = c_dst.n
    prep_dc = c_dc.n

    wt(SY, c_ac)
    iffs = []
    rec.items["sync"].append(("iff", ctrli_sb[0:1, 1:2], iffs))
    rec.items["sync"].append(("poll", c_po.sem_for("sync"), flg[0:1, 0:1], mbox_f[0:1, 0:1], ctrli_sb[0:1, 0:1], 0))
    c_ch.dma(SY.dma_start(out=stateio[:, :], in_=mbox_s[:, :]))
    rec.items["sync"].append(("endif", iffs))
    wtreg(VE, c_ch, ctrli_sb[0:1, 2:3])
    VE.scalar_tensor_tensor(out=Ssb[:, :], in0=stateio[:, :], scalar=ctrl_sb[:, 0:1],
                            in1=Ssb[:, :], op0=ALU.mult, op1=ALU.add)
    c_sc.inc(VE.tensor_copy(out=Sbf[:, :], in_=Ssb[:, :]))

    sc_vi = A3[:, 8192:12288]
    sc_kw = A3[:, 0:4096]
    sc_kc = A3[:, 4096:6144]
    sc_at = A3[:, 6144:8192]
    sc_qg = A3[:, 12288:14336]
    for cg in range(NSC):
        st, cch = cg // NCS, cg % NCS
        wt(GP, c_sc)
        wt(GP, c_dst, prep_dst)
        r0 = st * KH
        src = lambda t, p: t[r0:r0 + KH, :].rearrange("r (p c h t) -> p c r h t", p=p, c=NCS, h=2)[:, cch, :, :, :]
        dst = lambda t, p: t[0:p, :].rearrange("p (r h t) -> p r h t", r=KH, h=2)
        c_sl.dma(GP.dma_start(out=dst(sc_vi, 64), in_=src(stash_vi, 64)))
        c_sl.dma(GP.dma_start(out=dst(sc_kw, 64), in_=src(stash_kw, 64)))
        c_sl.dma(GP.dma_start(out=dst(sc_kc, 128), in_=src(stash_kc, 128)))
        c_sl.dma(GP.dma_start(out=dst(sc_at, 64), in_=src(stash_at, 64)))
        c_sl.dma(GP.dma_start(out=dst(sc_qg, 128), in_=src(stash_qg, 128)))
        if cg == 0:
            dbg_dump(0, sc_vi[0:64, 31 * 128:32 * 128], c_sl)
            dbg_dump(1, sc_kw[0:64, 31 * 128:32 * 128], c_sl)
            dbg_dump(2, sc_kc[:, 31 * 64:32 * 64], c_sl)
            dbg_dump(5, sc_at[0:64, 31 * 64:32 * 64], c_sl)
            dbg_dump(6, sc_qg[:, 31 * 64:32 * 64], c_sl)
        vnew8 = A3[0:64, 14336:15360]
        for bb in range(4):
            hs = list(range(bb * 8, bb * 8 + 8))
            wt(TE, c_sl)
            wt(TE, c_sc)
            for i, h in enumerate(hs):
                pv = (psQ[2] if i < 4 else psQ[3])[0:64, (i % 4) * 128:(i % 4) * 128 + 128]
                c_smm.inc(TE.matmul(pv, identb[0:64, 0:64],
                                    sc_vi[0:64, h * 128:(h + 1) * 128],
                                    start=(i % 4 == 0), stop=False))
                c_smm.inc(TE.matmul(pv, sc_kc[:, h * 64:(h + 1) * 64],
                                    Sbf[:, h * 128:(h + 1) * 128],
                                    start=False, stop=(i % 4 == 3)))
            wt(SC, c_smm)
            SC.activation(out=vnew8[:, 0:512], in_=psQ[2][0:64, 0:512], func=AF.Copy)
            c_sc.inc(SC.activation(out=vnew8[:, 512:1024], in_=psQ[3][0:64, 0:512], func=AF.Copy))
            wt(TE, c_sc)
            for i, h in enumerate(hs):
                vsl = vnew8[0:64, i * 128:(i + 1) * 128]
                c_smm.inc(TE.matmul(psX[:, i * 64:i * 64 + 64], Sbf[:, h * 128:(h + 1) * 128],
                                    sc_qg[:, h * 64:(h + 1) * 64], start=(i == 0), stop=False))
                c_smm.inc(TE.matmul(psX[:, i * 64:i * 64 + 64], vsl,
                                    sc_at[0:64, h * 64:(h + 1) * 64], start=False,
                                    stop=(i == 7)))
                ps = (psT if i < 4 else psW)[:, (i % 4) * 128:(i % 4) * 128 + 128]
                c_smm.inc(TE.matmul(ps, sc_kw[0:64, h * 128:(h + 1) * 128], vsl,
                                    start=(i % 4 == 0), stop=(i % 4 == 3)))
            wt(SC, c_smm)
            wt(SC, c_dst)
            c_sc.inc(SC.activation(out=corehold[:, bb * 512:(bb + 1) * 512],
                                   in_=psX[:, 0:512], func=AF.Copy))
            wt(VE, c_smm)
            for i, h in enumerate(hs):
                ps = (psT if i < 4 else psW)[:, (i % 4) * 128:(i % 4) * 128 + 128]
                c_sc.inc(VE.scalar_tensor_tensor(out=Ssb[:, h * 128:(h + 1) * 128],
                                                 in0=Ssb[:, h * 128:(h + 1) * 128],
                                                 scalar=lamS[:, h * NSC + cg:h * NSC + cg + 1],
                                                 in1=ps, op0=ALU.mult, op1=ALU.add))
            wt(GP, c_sc)
            c_sc.inc(GP.tensor_copy(out=Sbf[:, bb * 1024:(bb + 1) * 1024],
                                    in_=Ssb[:, bb * 1024:(bb + 1) * 1024]))
            if cg == 0 and h == 31:
                dbg_dump(7, vnew_sb[:, :], c_sc)
                dbg_dump(8, Ssb[:, 31 * 128:32 * 128], c_sc)
        wt(SY, c_sc)
        c_dst.dma(SY.dma_start(out=core_scr[:, cg * C:(cg + 1) * C].rearrange("(h p) t -> p h t", p=128),
                               in_=corehold[:, :].rearrange("p (h t) -> p h t", h=VH)))
        if cg == 0:
            dbg_dump(10, lamS[:, 0:512], c_sc)
            dbg_dump(11, lamS[:, 512:1024], c_sc)
    scan_sc = c_sc.snap()
    scan_dst = c_dst.n

    iffs2 = []
    wt(GP, c_sc, scan_sc)
    rec.items["gpsimd"].append(("iff", ctrli_sb[0:1, 3:4], iffs2))
    c_po.dma(GP.dma_start(out=mbox_s[:, :], in_=Ssb[:, :]))
    rec.items["gpsimd"].append(("wait", c_po.sem_for("gpsimd"), c_po.n_of("gpsimd")))
    c_po.dma(GP.dma_start(out=mbox_f[0:1, 0:1], in_=ctrli[0:1, 0:1]))
    rec.items["gpsimd"].append(("wait", c_po.sem_for("gpsimd"), c_po.n_of("gpsimd")))
    rec.items["gpsimd"].append(("endif", iffs2))

    wt(SY, c_dst, scan_dst)
    c_f.dma(SY.dma_start(out=A1[:, 0:KE * SPAN].rearrange("p (k t) -> p k t", k=KE),
                         in_=xn_scr[:, :].rearrange("(k p) t -> p k t", p=128)))
    xnf = lambda k: A1[:, k * SPAN:(k + 1) * SPAN]
    wfb = [A3[:, 0:2048], A3[:, 2048:4096]]
    wfq = A3[:, 4096:6144]
    absw = A3[:, 6144:8192]
    s1t = A3[:, 8192:10240]
    gsum = A3[:, 10240:12288]
    gatesb = A3[:, 12288:14336]

    def quant(buf, cc_dma):
        wt(SC, cc_dma)
        wt(SC, c_fc)
        c_fc.inc(SC.activation(out=absw[:, :], in_=buf[:, :], func=AF.Abs))
        for qtr in range(4):
            wt(TE, c_fc)
            wt(TE, c_fmm)
            c_fmm.inc(TE.matmul(psW[:, 0:512], bdiagb[:, 0:128], absw[:, qtr * 512:(qtr + 1) * 512],
                                start=True, stop=True))
            wt(SC, c_fmm)
            c_fc.inc(SC.activation(out=gsum[:, qtr * 512:(qtr + 1) * 512], in_=psW[:, 0:512], func=AF.Copy))
        wt(VE, c_fc)
        VE.scalar_tensor_tensor(out=s1t[:, :], in0=gsum[:, :], scalar=QTHR / 64.0,
                                in1=buf[:, :], op0=ALU.mult, op1=ALU.is_lt)
        c_fc.inc(VE.scalar_tensor_tensor(out=absw[:, :], in0=gsum[:, :], scalar=-QTHR / 64.0,
                                         in1=buf[:, :], op0=ALU.mult, op1=ALU.is_gt))
        wt(GP, c_fc)
        c_fc.inc(GP.tensor_sub(out=s1t[:, :], in0=s1t[:, :], in1=absw[:, :]))
        wt(VE, c_fc)
        wt(VE, c_fmm)
        c_fc.inc(VE.scalar_tensor_tensor(out=wfq[:, :], in0=gsum[:, :], scalar=1.0 / 64.0,
                                         in1=s1t[:, :], op0=ALU.mult, op1=ALU.mult))

    fmm_hist = []
    wt(GP, c_sl)
    wt(GP, c_smm)
    wt(GP, c_sc)
    for j in range(64):
        for gu in range(2):
            hc = j + gu * 64
            idx = j * 2 + gu
            buf = wfb[idx % 2]
            if idx >= 2:
                wt(GP, c_fmm, fmm_hist[idx - 2])
            cwp(idx).dma(GP.dma_start(out=buf[:, :].rearrange("p (k c) -> p k c", k=KE),
                                      in_=wguT[:, hc * 128:(hc + 1) * 128].rearrange("(k p) c -> p k c", p=128)))
            quant(buf, cwp(idx))
            wt(TE, c_fc)
            wt(TE, c_fc)
            for nt in range(4):
                for k in range(KE):
                    c_fmm.inc(TE.matmul(psQ[nt][:, 0:512], wfq[:, k * 128:(k + 1) * 128],
                                        xnf(k)[:, nt * 512:(nt + 1) * 512],
                                        start=(k == 0), stop=(k == KE - 1)))
            fmm_hist.append(c_fmm.n)
            if gu == 0:
                wt(SC, c_fmm)
                for nt in range(4):
                    c_fc.inc(SC.activation(out=gatesb[:, nt * 512:(nt + 1) * 512],
                                           in_=psQ[nt][:, 0:512], func=AF.Silu))
            else:
                wt(VE, c_fmm)
                for nt in range(4):
                    c_fc.inc(VE.tensor_tensor(out=gatesb[:, nt * 512:(nt + 1) * 512],
                                              in0=gatesb[:, nt * 512:(nt + 1) * 512],
                                              in1=psQ[nt][:, 0:512], op=ALU.mult))
                wt(SY, c_fc)
                c_f.dma(SY.dma_start(out=h_scr[j * 128:(j + 1) * 128, :], in_=gatesb[:, 0:2048]))
                wt(SC, c_fc)
                c_fc.inc(SC.activation(out=s1t[:, 0:2048], in_=gatesb[:, 0:2048], func=AF.Square))
                wt(TE, c_fc)
                c_fmm.inc(TE.matmul(psT[0:1, 0:512], onesrow_b[:, 0:1], s1t[:, 0:512],
                                    start=(j == 0), stop=(j == 63)))
                c_fmm.inc(TE.matmul(psT[32:33, 0:512], onesrow_b[:, 0:1], s1t[:, 512:1024],
                                    start=(j == 0), stop=(j == 63), tile_position=(0, 32)))
                c_fmm.inc(TE.matmul(psT[64:65, 0:512], onesrow_b[:, 0:1], s1t[:, 1024:1536],
                                    start=(j == 0), stop=(j == 63), tile_position=(0, 64)))
                c_fmm.inc(TE.matmul(psX[0:1, 0:512], onesrow_b[:, 0:1], s1t[:, 1536:2048],
                                    start=(j == 0), stop=(j == 63)))
    rsrow = A2F[0:1, 4099:6147]
    wt(SC, c_fmm)
    for i, ps in enumerate((psT[0:1, 0:512], psT[32:33, 0:512], psT[64:65, 0:512], psX[0:1, 0:512])):
        SC.activation(out=rsrow[0:1, i * 512:(i + 1) * 512], in_=ps, func=AF.Ln,
                      scale=1.0 / HALF_H, bias=epsffn_c[0:1, 0:1])
    c_fc.inc(SC.activation(out=rstdrow[0:1, :], in_=rsrow[0:1, :], func=AF.Exp, scale=-0.5))

    for dc in range(KE):
        for half in range(4):
            idx = dc * 4 + half
            buf = wfb[idx % 2]
            if idx >= 2:
                wt(GP, c_f, None)
            cwp(idx).dma(GP.dma_start(out=buf[:, :].rearrange("p (k c) -> p k c", k=KE),
                                      in_=wprojT[half * 2048:(half + 1) * 2048, dc * 128:(dc + 1) * 128]
                                      .rearrange("(k p) c -> p k c", p=128)))
            quant(buf, cwp(idx))
            wt(SY, c_fc)
            c_f.dma(SY.dma_start(out=wpq_scr[half * 2048:(half + 1) * 2048, dc * 128:(dc + 1) * 128]
                                 .rearrange("(k p) c -> p k c", p=128),
                                 in_=wfq[:, :].rearrange("p (k c) -> p k c", k=KE)))
    quant_f = c_f.n

    wp64 = A3[:, 0:8192]
    gu_fmm = c_fmm.n
    for nt in range(4):
        wt(SY, c_f, quant_f)
        wt(SY, c_fmm, gu_fmm)
        wt(SY, c_emm)
        c_f.dma(SY.dma_start(out=A1[:, 0:64 * 512].rearrange("p (k t) -> p k t", k=64),
                             in_=h_scr[:, nt * 512:(nt + 1) * 512].rearrange("(k p) t -> p k t", p=128)))
        wt(TE, c_f)
        c_emm.inc(TE.matmul(psW[:, 0:512], ones1b[:, 0:128], rstdrow[0:1, nt * 512:(nt + 1) * 512],
                            start=True, stop=True))
        wt(VE, c_emm)
        c_ec.inc(VE.tensor_copy(out=varb[:, 0:512], in_=psW[:, 0:512]))
        for dc in range(KE):
            wt(GP, c_emm)
            if nt == 0 and dc == 0:
                wt(GP, c_f)
                wt(GP, c_fmm)
                wt(GP, c_fc)
            c_w.dma(GP.dma_start(out=wp64[:, :].rearrange("p (k c) -> p k c", k=64),
                                 in_=wpq_scr[:, dc * 128:(dc + 1) * 128].rearrange("(k p) c -> p k c", p=128)))
            wt(TE, c_w)
            wt(TE, c_ec)
            for k in range(64):
                c_emm.inc(TE.matmul(psQ[0][:, 0:512], wp64[:, k * 128:(k + 1) * 128],
                                    A1[:, k * 512:(k + 1) * 512], start=(k == 0), stop=(k == 63)))
            wt(VE, c_emm)
            c_ec.inc(VE.tensor_tensor(out=cv0[:, 0:512], in0=psQ[0][:, 0:512], in1=varb[:, 0:512], op=ALU.mult))
            wt(SC, c_ec)
            c_ec.inc(SC.activation(out=gatesb[:, dc * 128:(dc + 1) * 128] if False else sqv[:, 0:512],
                                   in_=cv0[:, 0:512], func=AF.Silu))
            wt(SY, c_ec)
            c_f.dma(SY.dma_start(out=g_scr[dc * 128:(dc + 1) * 128, nt * 512:(nt + 1) * 512],
                                 in_=sqv[:, 0:512]))
    ffn_f = c_f.n
    ffn_ec = c_ec.n

    woutb = [A3[:, 8192 + i * 4096:8192 + (i + 1) * 4096] for i in range(1)]
    gat = lambda h: A1[:, h * SUB:(h + 1) * SUB]
    zslot = lambda h: A1[:, (32 + h) * SUB:(33 + h) * SUB]
    for st in range(NSUB):
        t0 = st * SUB
        wt(SY, c_f, ffn_f)
        wt(SY, c_ec)
        wt(SY, c_emm)
        c_e.dma(SY.dma_start(out=A1[:, 0:32 * SUB].rearrange("p (s t) -> p s t", s=32),
                             in_=core_scr[:, t0:t0 + SUB].rearrange("(s p) t -> p s t", p=128)))
        c_e.dma(SY.dma_start(out=A1[:, 32 * SUB:64 * SUB].rearrange("p (s t) -> p s t", s=32),
                             in_=z_scr[:, t0:t0 + SUB].rearrange("(s p) t -> p s t", p=128)))
        for h in range(VH):
            wt(SC, c_e)
            wt(SC, c_ec)
            wt(SC, c_emm)
            c_ec.inc(SC.activation(out=sqv[:, 0:SUB], in_=gat(h), func=AF.Square))
            wt(TE, c_ec)
            c_emm.inc(TE.matmul(psX[:, 0:SUB], onesb[:, 0:128], sqv[:, 0:SUB], start=True, stop=True))
            wt(SC, c_emm)
            SC.activation(out=varb[:, 0:SUB], in_=psX[:, 0:SUB], func=AF.Ln, scale=1.0 / 128.0,
                          bias=epsl2[:, 0:1])
            SC.activation(out=varb[:, 0:SUB], in_=varb[:, 0:SUB], func=AF.Exp, scale=-0.5)
            c_ec.inc(SC.activation(out=m2b[:, 0:SUB], in_=zslot(h), func=AF.Silu))
            wt(VE, c_ec)
            VE.scalar_tensor_tensor(out=gat(h), in0=gat(h), scalar=normw_sb[:, 0:1],
                                    in1=varb[:, 0:SUB], op0=ALU.mult, op1=ALU.mult)
            c_ec.inc(VE.tensor_tensor(out=gat(h), in0=gat(h), in1=m2b[:, 0:SUB], op=ALU.mult))
        if st == 0:
            dbg_dump(15, A1[:, 1 * SUB:1 * SUB + 512], c_e)
            dbg_dump(4, zslot(31)[:, 0:512], c_e)
            dbg_dump(12, varb[:, 0:512], c_ec)
            if dbg_h is not None:
                wt(GP, c_dst)
                c_out.dma(GP.dma_start(out=dbg_h[:, 16 * 512:17 * 512],
                                       in_=core_scr[128:256, 0:512]))
                wt(GP, c_dst)
                c_out.dma(GP.dma_start(out=dbg_h[:, 3 * 512:4 * 512],
                                       in_=core_scr[31 * 128:32 * 128, 0:512]))
        for ec in range(KE):
            wt(GP, c_emm)
            c_w.dma(GP.dma_start(out=woutb[0][:, :].rearrange("p (k c) -> p k c", k=32),
                                 in_=wout[:, ec * 128:(ec + 1) * 128].rearrange("(k p) c -> p k c", p=128)))
            wt(TE, c_w)
            wt(TE, c_ec)
            for h in range(VH):
                c_emm.inc(TE.matmul(psQ[1][:, 0:512], woutb[0][:, h * 128:(h + 1) * 128], gat(h),
                                    start=(h == 0), stop=(h == VH - 1)))
            wt(SY, c_ec)
            c_e.dma(SY.dma_start(out=xf[:, 0:SUB], in_=xT[ec * 128:(ec + 1) * 128, 3 + t0:3 + t0 + SUB]))
            c_e.dma(SY.dma_start(out=xbv[:, 0:SUB], in_=g_scr[ec * 128:(ec + 1) * 128, t0:t0 + SUB]))
            wt(VE, c_emm)
            wt(VE, c_e)
            wt(VE, c_out)
            wt(VE, c_ec)
            c_ec.inc(VE.tensor_copy(out=m2b[:, 0:SUB], in_=psQ[1][:, 0:512]))
            if st == 0 and ec == 0:
                dbg_dump(17, A1[:, 31 * SUB:31 * SUB + 512], c_ec)
                dbg_dump(18, m2b[:, 0:512], c_ec)
                dbg_dump(19, woutb[0][:, 0:512], c_ec)
            VE.tensor_tensor(out=cv0[:, 0:SUB], in0=m2b[:, 0:SUB], in1=xbv[:, 0:SUB], op=ALU.mult)
            c_ec.inc(VE.tensor_tensor(out=cv0[:, 0:SUB], in0=cv0[:, 0:SUB], in1=xf[:, 0:SUB], op=ALU.add))
            wt(SY, c_ec)
            c_out.dma(SY.dma_start(out=out_h[ec * 128:(ec + 1) * 128, t0:t0 + SUB], in_=cv0[:, 0:SUB]))
    for _e, _s in c_out.sub.items():
        rec.items["gpsimd"].append(("wait", _s[0], _s[1]))
        rec.items["sync"].append(("wait", _s[0], _s[1]))

    with es, nc.Block() as block:
        rec.emit(block)
    return nc


_BUILT = None
_BUILT_DBG = None


def _consts_host():
    cb = np.zeros((128, 1024), np.float32)
    cb[:, 0:128] = np.eye(128)
    s = np.arange(64)
    m1 = (s[:, None] > s[None, :]).astype(np.float32) * 30000.0
    cb[0:64, 128:192] = m1
    cb[64:128, 128:192] = m1
    cb[63, 192] = 1.0
    cb[127, 192] = 1.0
    cb[:, 193] = 1.0
    cb[:, 194] = F32EPS
    for c in range(NCS):
        cb[c, 208 + c * C:208 + (c + 1) * C] = 1.0
    bd = np.zeros((128, 128), np.float32)
    bd[0:64, 0:64] = 1.0
    bd[64:128, 64:128] = 1.0
    cb[:, 720:848] = bd
    return cb


def kernel(L0_attn_hidden_states, L0_attn_w_qkvz, L0_attn_w_ba, L0_attn_conv_w,
           L0_attn_conv_b, L0_attn_A_log, L0_attn_dt_bias, L0_attn_norm_weight,
           L0_attn_w_out, L0_ffn_w_gate_up, L0_ffn_w_proj):
    global _BUILT
    x = np.asarray(L0_attn_hidden_states)
    B, L, _ = x.shape
    ncore = 8
    spans = 4
    cb = _consts_host()
    wguT = np.ascontiguousarray(np.asarray(L0_ffn_w_gate_up).T)
    wprojT = np.ascontiguousarray(np.asarray(L0_ffn_w_proj).T)
    convw_h = np.zeros((128, 256), np.float32)
    cw = np.asarray(L0_attn_conv_w)
    for ch in range(64):
        for j in range(4):
            convw_h[:, ch * 4 + j] = cw[j, ch * 128:(ch + 1) * 128]
    convb_h = np.ascontiguousarray(np.asarray(L0_attn_conv_b).reshape(64, 128).T)
    alog_h = np.asarray(L0_attn_A_log).reshape(32, 1)
    dtb_h = np.asarray(L0_attn_dt_bias).reshape(32, 1)
    normw_h = np.asarray(L0_attn_norm_weight).reshape(128, 1)
    nonce = int(np.random.randint(1, 2 ** 30))

    nonce2 = int(np.random.randint(1, 2 ** 30))
    wqkvz_h = np.asarray(L0_attn_w_qkvz)
    wba_h = np.asarray(L0_attn_w_ba)
    cb_full = np.asarray(L0_attn_conv_b)

    in_maps = []
    for core in range(ncore):
        b = core // 4
        pos = core % 4
        t0 = pos * SPAN
        xs = np.zeros((SPAN + 3, E), np.float32)
        xs[3:] = x[b, t0:t0 + SPAN]
        if t0 > 0:
            xs[0:3] = x[b, t0 - 3:t0]
        ci = np.zeros((1, 16), np.int32)
        ci[0, 0] = nonce
        ci[0, 1] = 1 if pos in (1, 3) else 0
        ci[0, 2] = 16 if pos in (1, 3) else 0
        ci[0, 3] = 1 if pos in (0, 2) else 0
        ci[0, 4] = nonce2
        ci[0, 5] = 1 if pos in (0, 2) else 0
        ci[0, 6] = 16 if pos in (0, 2) else 0
        ci[0, 7] = 1 if pos in (1, 3) else 0
        cf = np.zeros((128, 8), np.float32)
        cf[:, 0] = 1.0 if pos in (1, 3) else 0.0
        cf[:, 1] = 1.0 if pos in (0, 2) else 0.0
        hh0 = 0 if pos in (0, 2) else 16
        kh0 = hh0 // 2
        xp = np.zeros((4096 + 3, E), np.float32)
        xp[3:] = x[b, 0:4096]
        wkvp = np.zeros((E, 3072), np.float32)
        convwp = np.zeros((128, 96), np.float32)
        convbp = np.zeros((128, 24), np.float32)
        if pos in (2, 3):
            wkvp[:, 0:1024] = wqkvz_h[:, 2048 + kh0 * 128:2048 + kh0 * 128 + 1024]
            wkvp[:, 1024:3072] = wqkvz_h[:, 4096 + hh0 * 128:4096 + hh0 * 128 + 2048]
            grps = [16 + kh0 + i for i in range(8)] + [32 + hh0 + i for i in range(16)]
            for gi, g128 in enumerate(grps):
                for j in range(4):
                    convwp[:, gi * 4 + j] = cw[j, g128 * 128:(g128 + 1) * 128]
                convbp[:, gi] = cb_full[g128 * 128:(g128 + 1) * 128]
        wbap = np.zeros((E, 64), np.float32)
        wbap[:, 0:16] = wba_h[:, hh0:hh0 + 16]
        wbap[:, 32:48] = wba_h[:, 32 + hh0:32 + hh0 + 16]
        alogp = alog_h[hh0:hh0 + 16]
        dtbp = dtb_h[hh0:hh0 + 16]
        in_maps.append({
            "xT": np.ascontiguousarray(xs.T),
            "wqkvz": wqkvz_h,
            "wba": wba_h,
            "wout": np.asarray(L0_attn_w_out),
            "wguT": wguT, "wprojT": wprojT,
            "convw": convw_h, "convb": convb_h,
            "alog": alog_h, "dtb": dtb_h, "normw": normw_h,
            "consts": cb, "ctrlf": cf, "ctrli": ci,
            "xPT": np.ascontiguousarray(xp.T),
            "wkvP": wkvp, "wbaP": wbap,
            "convwP": convwp, "convbP": convbp,
            "alogP": np.ascontiguousarray(alogp), "dtbP": np.ascontiguousarray(dtbp),
        })
    import os
    want_dbg = bool(os.environ.get("KDBG"))
    global _BUILT_DBG
    if want_dbg:
        if _BUILT_DBG is None:
            _BUILT_DBG = build(dbg=True)
        nc_use = _BUILT_DBG
    else:
        if _BUILT is None:
            _BUILT = build()
        nc_use = _BUILT
    res = run_bass_kernel_spmd(nc_use, in_maps, core_ids=list(range(ncore)))
    if want_dbg:
        np.save("/tmp/kdbg.npy", np.stack([res.results[c]["dbg"] for c in range(ncore)]))
    out = np.zeros((B, L, E), np.float32)
    for core in range(ncore):
        b = core // 4
        pos = core % 4
        out[b, pos * SPAN:(pos + 1) * SPAN] = res.results[core]["o"].T
    return out


# version 14
# speedup vs baseline: 1.2106x; 1.2106x over previous
import numpy as np
import ml_dtypes
import concourse.bass as bass
import concourse.mybir as mybir
from concourse.bass_utils import run_bass_kernel_spmd
from contextlib import ExitStack

F32 = mybir.dt.float32
BF16 = mybir.dt.bfloat16
I32 = mybir.dt.int32
AF = mybir.ActivationFunctionType
ALU = mybir.AluOpType

E = 2048
KE = 16
VH = 32
KH = 16
C = 64
SPAN = 2048
SUB = 512
NSUB = SPAN // SUB
NCS = SUB // C
NSC = SPAN // C
NQKV = 8192
NZ = 4096
HFFN = 16384
HALF_H = 8192
LN_DKI = float(np.log(128.0 ** -0.5))
QTHR = 0.5009765625
F32EPS = float(np.finfo(np.float32).eps)


class Rec:
    def __init__(self):
        self.items = {k: [] for k in ("sync", "gpsimd", "vector", "scalar", "tensor")}

    def emit(self, block):
        def mk(name):
            def body(eng):
                stack = []
                for it in self.items[name]:
                    kind = it[0]
                    if kind == "wait":
                        eng.wait_ge(it[1], it[2])
                    elif kind == "waitreg":
                        eng.wait_ge(it[1], eng.value_load(it[2]))
                    elif kind == "iff":
                        g = eng.If(eng.value_load(it[1]))
                        g.__enter__()
                        stack.append(g)
                    elif kind == "endif":
                        stack.pop().__exit__(None, None, None)
                    elif kind == "poll":
                        sem, flgap, srcap, nonceap, tgt0 = it[1:]
                        nreg = eng.value_load(nonceap)
                        self._pollid = getattr(self, "_pollid", 0) + 1
                        with eng.register(f"ptgt{self._pollid}") as tgt:
                            eng.reg_mov(tgt, tgt0 + 16)
                            with eng.While(lambda: eng.value_load(flgap) != nreg):
                                eng.nop(cycle_cnt=2000)
                                eng.dma_start(out=flgap, in_=srcap).then_inc(sem, 16)
                                eng.wait_ge(sem, tgt)
                                eng.reg_add(tgt, tgt, 16)
                    else:
                        d = it[1]
                        ins = getattr(eng, d["meth"])(*d["args"], **d["kwargs"])
                        for sem, n in d["incs"]:
                            ins.then_inc(sem, n)
            return body
        block.sync(mk("sync"))
        block.gpsimd(mk("gpsimd"))
        block.vector(mk("vector"))
        block.scalar(mk("scalar"))
        block.tensor(mk("tensor"))


SELF = {}
LASTC = {}
SELF_ENGS = ("vector", "scalar", "gpsimd")


class Eng:
    def __init__(self, rec, name):
        self._rec, self._name = rec, name

    def __getattr__(self, meth):
        rec, name = self._rec, self._name

        def call(*args, **kwargs):
            if name in SELF_ENGS and SELF:
                lastc = LASTC.get(name)
                if lastc is not None:
                    if lastc["sync"] is None:
                        SELF["ctr"].inc_d(lastc)
                    rec.items[name].append(("wait",) + lastc["sync"])
            entry = ["op", {"meth": meth, "args": args, "kwargs": kwargs,
                            "incs": [], "eng": name, "sync": None}]
            rec.items[name].append(entry)
            if name in SELF_ENGS and SELF and meth != "dma_start":
                LASTC[name] = entry[1]
            return entry
        return call


class Ctr:

    def __init__(self, es, nc, name):
        self.es, self.nc, self.name = es, nc, name
        self.sub = {}

    def _get(self, eng):
        if eng not in self.sub:
            self.sub[eng] = [self.es.enter_context(self.nc.semaphore(f"{self.name}_{eng[:2]}")), 0]
        return self.sub[eng]

    def inc_d(self, d, n=1):
        s = self._get(d["eng"])
        s[1] += n
        d["incs"].append((s[0], n))
        if d["sync"] is None:
            d["sync"] = (s[0], s[1])

    def inc(self, entry, n=1):
        self.inc_d(entry[1], n)

    def dma(self, entry):
        self.inc(entry, 16)

    @property
    def n(self):
        return sum(s[1] for s in self.sub.values())

    def snap(self):
        return {e: s[1] for e, s in self.sub.items()}

    def sem_for(self, eng):
        return self._get(eng)[0]

    def n_of(self, eng):
        return self._get(eng)[1]


def build(dbg=False):
    nc = bass.Bass("TRN2", target_bir_lowering=False)
    es = ExitStack()
    D = lambda name, sh, dt=F32: nc.dram_tensor(name, sh, dt, kind="ExternalInput").ap()
    xT = D("xT", [E, SPAN + 3])
    wqkvz = D("wqkvz", [E, 12288])
    wba = D("wba", [E, 64])
    xPT = D("xPT", [E, 4096 + 3])
    wkvP = D("wkvP", [E, 3072])
    wbaP = D("wbaP", [E, 64])
    convwP = D("convwP", [128, 96])
    convbP = D("convbP", [128, 24])
    alogP = D("alogP", [16, 1])
    dtbP = D("dtbP", [16, 1])
    wout = D("wout", [NZ, E])
    wguT = D("wguT", [E, HFFN])
    wprojT = D("wprojT", [HALF_H, E])
    convw = D("convw", [128, 256])
    convb = D("convb", [128, 64])
    alog = D("alog", [32, 1])
    dtb = D("dtb", [32, 1])
    normw = D("normw", [128, 1])
    consts = D("consts", [128, 1024])
    ctrlf = D("ctrlf", [128, 8])
    ctrli = D("ctrli", [1, 16], I32)
    out_h = nc.dram_tensor("o", [E, SPAN], F32, kind="ExternalOutput").ap()
    dbg_h = nc.dram_tensor("dbg", [128, 20 * 512], F32, kind="ExternalOutput").ap() if dbg else None

    xn_scr = nc.dram_tensor("xn_scr", [E, SPAN], BF16).ap()
    qkv_scr = nc.dram_tensor("qkv_scr", [NQKV, SPAN], BF16).ap()
    z_scr = nc.dram_tensor("z_scr", [NZ, SPAN], BF16).ap()
    core_scr = nc.dram_tensor("core_scr", [NZ, SPAN], BF16).ap()
    h_scr = nc.dram_tensor("h_scr", [HALF_H, SPAN], BF16).ap()
    g_scr = nc.dram_tensor("g_scr", [E, SPAN], BF16).ap()
    wpq_scr = nc.dram_tensor("wpq_scr", [HALF_H, E], BF16).ap()
    gstat_scr = nc.dram_tensor("gstat_scr", [64, SPAN], F32).ap()
    kvP_scr = nc.dram_tensor("kvP_scr", [3072, 4096], BF16).ap()
    gstatP_scr = nc.dram_tensor("gstatP_scr", [32, 4096], F32).ap()
    stash_vi = nc.dram_tensor("stash_vi", [NSUB * KH, 64 * NCS * 2 * 128], BF16).ap()
    stash_kw = nc.dram_tensor("stash_kw", [NSUB * KH, 64 * NCS * 2 * 128], BF16).ap()
    stash_kc = nc.dram_tensor("stash_kc", [NSUB * KH, 128 * NCS * 2 * 64], BF16).ap()
    stash_at = nc.dram_tensor("stash_at", [NSUB * KH, 64 * NCS * 2 * 64], BF16).ap()
    stash_qg = nc.dram_tensor("stash_qg", [NSUB * KH, 128 * NCS * 2 * 64], BF16).ap()
    mbox_s = nc.dram_tensor("mbox_s", [128, VH * 128], F32, addr_space="Shared").ap()
    mbox_f = nc.dram_tensor("mbox_f", [1, 8], I32, addr_space="Shared").ap()

    S_ = lambda n, sh, dt: es.enter_context(nc.sbuf_tensor(n, sh, dt))
    cons = S_("cons", [128, 1024], F32)
    identf = cons[:, 0:128]
    bigLS = cons[:, 128:192]
    e63c = cons[:, 192:193]
    onecol = cons[:, 193:194]
    epsffn_c = cons[:, 194:195]
    sel8 = cons[0:NCS, 208:208 + NCS * C]
    bdiagf = cons[:, 720:848]
    identb = S_("identb", [128, 128], BF16)
    bdiagb = S_("bdiagb", [128, 128], BF16)
    onesb = S_("onesb", [128, 128], BF16)
    ones1b = S_("ones1b", [1, 128], BF16)
    nones8 = S_("nones8", [NCS, 128], F32)
    epsln = S_("epsln", [128, 1], F32)
    epsl2 = S_("epsl2", [128, 1], F32)
    lnq = S_("lnq", [128, 1], F32)
    wba_sb = S_("wba_sb", [128, KE * 64], BF16)
    convw_sb = S_("convw_sb", [128, 256], F32)
    convb_sb = S_("convb_sb", [128, 64], F32)
    alog_sb = S_("alog_sb", [32, 1], F32)
    dtb_sb = S_("dtb_sb", [32, 1], F32)
    nea_sb = S_("nea_sb", [32, 1], F32)
    convwP_sb = S_("convwP_sb", [128, 96], F32)
    convbP_sb = S_("convbP_sb", [128, 24], F32)
    alogP_sb = S_("alogP_sb", [16, 1], F32)
    dtbP_sb = S_("dtbP_sb", [16, 1], F32)
    neaP_sb = S_("neaP_sb", [16, 1], F32)
    normw_sb = S_("normw_sb", [128, 1], F32)
    ctrl_sb = S_("ctrl_sb", [128, 8], F32)
    ctrli_sb = S_("ctrli_sb", [1, 16], I32)

    Ssb = S_("Ssb", [128, VH * 128], F32)
    Sbf = S_("Sbf", [128, VH * 128], BF16)
    flg = S_("flg", [1, 8], I32)
    lamS = S_("lamS", [128, VH * NSC], F32)

    A1 = S_("arena1", [128, KE * (SPAN + 3)], BF16)
    A2F = S_("arena2f", [128, 6400], F32)
    A3 = S_("arena3", [128, 15360], BF16)
    b_sp = A2F[0:32, 0:2048]
    a_sp = A2F[0:32, 2048:4096]
    gtmp = A2F[0:32, 4096:6144]
    stateio = A2F[:, 2048:6144]
    decT_t = S_("decT_t", [64, 2 * NCS * C], BF16)

    xf = S_("xf", [128, SUB + 3], F32)
    sqv = S_("sqv", [128, SUB + 3], BF16)
    xbv = S_("xbv", [128, SUB + 3], BF16)
    meanb = S_("meanb", [128, SUB + 3], F32)
    varb = S_("varb", [128, SUB + 3], F32)
    m2b = S_("m2b", [128, SUB + 3], F32)

    gch = S_("gch", [NCS, VH * C], F32)
    gchr = S_("gchr", [NCS, VH * C], F32)
    bch = S_("bch", [NCS, VH * C], BF16)
    kwch = S_("kwch", [NCS, VH * C], BF16)
    zeros_ch = S_("zeros_ch", [NCS, C], F32)
    dp = S_("dp", [NCS, NCS * C], F32)
    bT2 = S_("bT2", [128, KH * NCS], F32)
    nbT2 = S_("nbT2", [128, KH * NCS], F32)
    bgT2 = S_("bgT2", [128, KH * NCS], F32)
    kwT2 = S_("kwT2", [128, KH * NCS], F32)
    grT2 = S_("grT2", [128, KH * NCS], F32)
    lrow = S_("lrow", [1, VH * NCS], BF16)

    Ablk = S_("Ablk", [128, 128], BF16)
    AblkT = S_("AblkT", [128, 128], BF16)
    Bv0 = S_("Bv0", [128, 128], BF16)
    Bv1 = S_("Bv1", [128, 128], BF16)
    BvT0 = S_("BvT0", [128, 128], BF16)
    BvT1 = S_("BvT1", [128, 128], BF16)
    Ptl = S_("Ptl", [128, 128], BF16)
    Rblk = S_("Rblk", [128, 256], BF16)
    kcdT_sb = S_("kcdT_sb", [64, 128], BF16)
    vnew_sb = S_("vnew_sb", [64, 128], BF16)
    Dmisc = S_("Dmisc", [128, VH * C], BF16)
    corehold = Dmisc[:, :]
    rstdrow = Dmisc[0:1, 0:SPAN]
    onesrow_b = S_("onesrow_b", [128, 1], BF16)

    P_ = lambda n, sh, dt: es.enter_context(nc.psum_tensor(n, sh, dt))
    psQ = [P_(f"psQ{i}", [128, 512], F32) for i in range(4)]
    psT = P_("psT", [128, 512], F32)
    psX = P_("psX", [128, 512], F32)
    psW = P_("psW", [128, 512], F32)
    psB16 = P_("psB16", [128, 1024], BF16)

    rec = Rec()
    SY, GP, VE, SC, TE = (Eng(rec, n) for n in ("sync", "gpsimd", "vector", "scalar", "tensor"))
    names = "ld x amm ac w w2 bmm bev bst d dc dmm dst sl smm sc f fmm fc e emm ec out ch po pp dg lg".split()
    ct = {n: Ctr(es, nc, n) for n in names}
    SELF.clear()
    LASTC.clear()
    SELF["ctr"] = Ctr(es, nc, "zz")
    (c_ld, c_x, c_amm, c_ac, c_w, c_w2, c_bmm, c_bev, c_bst, c_d, c_dc, c_dmm, c_dst,
     c_sl, c_smm, c_sc, c_f, c_fmm, c_fc, c_e, c_emm, c_ec, c_out, c_ch, c_po,
     c_pp, c_dg, c_lg) = (ct[n] for n in names)
    cwp = lambda i: c_w if i % 2 == 0 else c_w2

    def wt(eng, ctr, n=None):
        if n is None:
            for e, s in ctr.sub.items():
                rec.items[eng._name].append(("wait", s[0], s[1]))
        elif isinstance(n, dict):
            for e, v in n.items():
                rec.items[eng._name].append(("wait", ctr.sub[e][0], v))
        else:
            subs = list(ctr.sub.values())
            assert len(subs) <= 1, (ctr.name, list(ctr.sub))
            if subs:
                rec.items[eng._name].append(("wait", subs[0][0], n))

    def wtreg(eng, ctr, ap, src_eng="sync"):
        rec.items[eng._name].append(("waitreg", ctr.sem_for(src_eng), ap))

    def dbg_dump(slot, ap, ctr):
        if dbg_h is None:
            return
        wt(GP, ctr)
        c_out.dma(GP.dma_start(out=dbg_h[0:ap.shape[0], slot * 512:slot * 512 + ap.shape[1]], in_=ap))

    c_ld.dma(SY.dma_start(out=cons[:, :], in_=consts[:, :]))
    c_ld.dma(SY.dma_start(out=convw_sb[:], in_=convw[:, :]))
    c_ld.dma(SY.dma_start(out=convb_sb[:], in_=convb[:, :]))
    c_ld.dma(SY.dma_start(out=alog_sb[:], in_=alog[:, :]))
    c_ld.dma(SY.dma_start(out=dtb_sb[:], in_=dtb[:, :]))
    c_ld.dma(SY.dma_start(out=normw_sb[:], in_=normw[:, :]))
    c_ld.dma(SY.dma_start(out=ctrl_sb[:], in_=ctrlf[:, :]))
    c_ld.dma(SY.dma_start(out=ctrli_sb[:], in_=ctrli[:, :]))
    c_lg.dma(GP.dma_start(out=wba_sb[:, :].rearrange("p (k c) -> p k c", k=KE), in_=wbaP[:, :].rearrange("(k p) c -> p k c", p=128)))
    c_lg.dma(GP.dma_start(out=identb[:], in_=consts[:, 0:128]))
    c_lg.dma(GP.dma_start(out=bdiagb[:], in_=consts[:, 720:848]))
    c_ld.dma(SY.dma_start(out=convwP_sb[:], in_=convwP[:, :]))
    c_ld.dma(SY.dma_start(out=convbP_sb[:], in_=convbP[:, :]))
    c_ld.dma(SY.dma_start(out=alogP_sb[:], in_=alogP[:, :]))
    c_ld.dma(SY.dma_start(out=dtbP_sb[:], in_=dtbP[:, :]))
    wt(VE, c_ld)
    wt(VE, c_lg)
    VE.memset(onesb[:], 1.0)
    VE.memset(ones1b[:], 1.0)
    VE.memset(nones8[:], -1.0)
    VE.memset(epsln[:], 1e-5)
    VE.memset(epsl2[:], 1e-6)
    VE.memset(lnq[:], LN_DKI)
    VE.memset(zeros_ch[:], 0.0)
    VE.memset(Ssb[:], 0.0)
    VE.memset(Sbf[:], 0.0)
    VE.memset(stateio[:], 0.0)
    VE.memset(onesrow_b[:], 1.0)
    c_ac.inc(VE.memset(flg[:], 0))
    wt(SC, c_ld)
    wt(SC, c_lg)
    SC.activation(out=nea_sb[:], in_=alog_sb[:], func=AF.Exp)
    c_ac.inc(SC.mul(nea_sb[:], nea_sb[:], -1.0))
    SC.activation(out=neaP_sb[:], in_=alogP_sb[:], func=AF.Exp)
    c_ac.inc(SC.mul(neaP_sb[:], neaP_sb[:], -1.0))

    xnspP = lambda k: A1[:, k * (SPAN + 3):(k + 1) * (SPAN + 3)]
    qkP = lambda slot: A1[:, slot * SUB:(slot + 1) * SUB]
    wqP = [A3[:, i * 2048:(i + 1) * 2048] for i in range(2)]
    siltileP = A3[:, 4096:6144]
    rawtP = A2F[:, 0:2051]
    cv0P = A2F[:, 2051:4099]
    NTP = [(0, 512), (512, 512), (1024, 512), (1536, 512), (2048, 3)]
    bmm_histP = []
    for half in range(2):
        hb = half * 2048
        for st in range(NSUB):
            t0 = st * SUB
            x0 = 0 if st == 0 else 3 + t0
            cols = SUB + 3 if st == 0 else SUB
            parts = [(0, 512)] + ([(512, cols - 512)] if cols > 512 else [])
            for k in range(KE):
                wt(SY, c_ac)
                c_x.dma(SY.dma_start(out=xf[:, 0:cols], in_=xPT[k * 128:(k + 1) * 128, hb + x0:hb + x0 + cols]))
                wt(SC, c_x)
                wt(SC, c_amm)
                c_ac.inc(SC.activation(out=sqv[:, 0:cols], in_=xf[:, 0:cols], func=AF.Square))
                wt(VE, c_x)
                wt(VE, c_amm)
                if half == 1 and st == 0 and k == 0:
                    wt(VE, c_bmm)
                VE.tensor_copy(out=xbv[:, 0:cols], in_=xf[:, 0:cols])
                c_ac.inc(VE.tensor_copy(out=xnspP(k)[:, x0:x0 + cols], in_=xf[:, 0:cols]))
                wt(TE, c_ac)
                if half == 1 and st == 0 and k == 0:
                    wt(TE, c_bev)
                for pi, (o, w) in enumerate(parts):
                    pa = psX[:, 0:w] if pi == 0 else psT[:, 0:w]
                    pb = psW[:, 0:w] if pi == 0 else psT[:, 128:128 + w]
                    c_amm.inc(TE.matmul(pa, onesb[:, 0:128], xbv[:, o:o + w],
                                        start=(k == 0), stop=(k == KE - 1 and pi == 0)))
                    c_amm.inc(TE.matmul(pb, onesb[:, 0:128], sqv[:, o:o + w],
                                        start=(k == 0 and pi == 0), stop=(k == KE - 1)))
            wt(VE, c_amm)
            for pi, (o, w) in enumerate(parts):
                pa = psX[:, 0:w] if pi == 0 else psT[:, 0:w]
                pb = psW[:, 0:w] if pi == 0 else psT[:, 128:128 + w]
                VE.tensor_scalar(out=meanb[:, o:o + w], in0=pa, scalar1=1.0 / E, scalar2=None, op0=ALU.mult)
                VE.tensor_tensor(out=m2b[:, o:o + w], in0=meanb[:, o:o + w], in1=meanb[:, o:o + w], op=ALU.mult)
                c_ac.inc(VE.scalar_tensor_tensor(out=varb[:, o:o + w], in0=pb, scalar=1.0 / E,
                                                 in1=m2b[:, o:o + w], op0=ALU.mult, op1=ALU.subtract))
            wt(SC, c_ac)
            SC.activation(out=varb[:, 0:cols], in_=varb[:, 0:cols], func=AF.Ln, bias=epsln[:, 0:1])
            c_ac.inc(SC.activation(out=varb[:, 0:cols], in_=varb[:, 0:cols], func=AF.Exp, scale=-0.5))
            for k in range(KE):
                sl = xnspP(k)[:, x0:x0 + cols]
                wt(VE, c_ac)
                VE.tensor_tensor(out=sl, in0=sl, in1=meanb[:, 0:cols], op=ALU.subtract)
                c_ac.inc(VE.tensor_tensor(out=sl, in0=sl, in1=varb[:, 0:cols], op=ALU.mult))
            for k in range(KE):
                wt(TE, c_ac)
                c_amm.inc(TE.matmul(psX[0:64, 0:SUB], wba_sb[:, k * 64:(k + 1) * 64],
                                    xnspP(k)[:, 3 + t0:3 + t0 + SUB],
                                    start=(k == 0), stop=(k == KE - 1)))
            wt(SC, c_amm)
            wt(SC, c_bst)
            wt(SC, c_bev)
            SC.activation(out=b_sp[:, t0:t0 + SUB], in_=psX[0:32, 0:SUB], func=AF.Copy)
            c_ac.inc(SC.activation(out=a_sp[:, t0:t0 + SUB], in_=psX[32:64, 0:SUB], func=AF.Copy))
        wt(VE, c_ac)
        c_ac.inc(VE.tensor_scalar(out=gtmp[0:16, :], in0=a_sp[0:16, :], scalar1=dtbP_sb[0:16, 0:1],
                                  scalar2=None, op0=ALU.add))
        wt(SC, c_ac)
        c_ac.inc(SC.activation(out=b_sp[0:16, :], in_=b_sp[0:16, :], func=AF.Sigmoid))
        wt(SC, c_ac)
        c_ac.inc(SC.activation(out=a_sp[0:16, :], in_=gtmp[0:16, :], func=AF.Abs))
        SC.activation(out=a_sp[0:16, :], in_=a_sp[0:16, :], func=AF.Exp, scale=-1.0)
        c_ac.inc(SC.activation(out=a_sp[0:16, :], in_=a_sp[0:16, :], func=AF.Ln, bias=onecol[0:16, 0:1]))
        wt(VE, c_ac)
        VE.tensor_scalar(out=gtmp[0:16, :], in0=gtmp[0:16, :], scalar1=0.0, scalar2=None, op0=ALU.max)
        VE.tensor_tensor(out=a_sp[0:16, :], in0=a_sp[0:16, :], in1=gtmp[0:16, :], op=ALU.add)
        c_ac.inc(VE.tensor_scalar(out=a_sp[0:16, :], in0=a_sp[0:16, :],
                                  scalar1=neaP_sb[0:16, 0:1], scalar2=None, op0=ALU.mult))
        wt(SY, c_ac)
        c_bst.dma(SY.dma_start(out=gstatP_scr[0:16, hb:hb + 2048], in_=a_sp[0:16, :]))
        c_bst.dma(SY.dma_start(out=gstatP_scr[16:32, hb:hb + 2048], in_=b_sp[0:16, :]))
        for oc in range(24):
            buf = wqP[oc % 2]
            if len(bmm_histP) >= 2:
                wt(GP, c_bmm, bmm_histP[-2])
            cwp(oc).dma(GP.dma_start(out=buf[:, :].rearrange("p (k c) -> p k c", k=KE),
                                     in_=wkvP[:, oc * 128:(oc + 1) * 128].rearrange("(k p) c -> p k c", p=128)))
            wt(TE, cwp(oc))
            wt(TE, c_bev)
            for (o, w) in NTP:
                for k in range(KE):
                    ps = psQ[o // 512][:, 0:512] if w == 512 else psT[:, 0:3]
                    c_bmm.inc(TE.matmul(ps, buf[:, k * 128:(k + 1) * 128], xnspP(k)[:, o:o + w],
                                        start=(k == 0), stop=(k == KE - 1)))
            bmm_histP.append(c_bmm.n)
            wt(SC, c_bmm)
            wt(SC, c_bev)
            wt(SC, c_bst)
            for (o, w) in NTP:
                ps = psQ[o // 512][:, 0:w] if w == 512 else psT[:, 0:3]
                c_bev.inc(SC.activation(out=rawtP[:, o:o + w], in_=ps, func=AF.Copy))
            wt(VE, c_bev)
            VE.tensor_scalar(out=cv0P[:, 0:2048], in0=rawtP[:, 0:2048],
                             scalar1=convwP_sb[:, oc * 4:oc * 4 + 1], scalar2=None, op0=ALU.mult)
            VE.scalar_tensor_tensor(out=cv0P[:, 0:2048], in0=rawtP[:, 1:2049],
                                    scalar=convwP_sb[:, oc * 4 + 1:oc * 4 + 2],
                                    in1=cv0P[:, 0:2048], op0=ALU.mult, op1=ALU.add)
            VE.scalar_tensor_tensor(out=cv0P[:, 0:2048], in0=rawtP[:, 2:2050],
                                    scalar=convwP_sb[:, oc * 4 + 2:oc * 4 + 3],
                                    in1=cv0P[:, 0:2048], op0=ALU.mult, op1=ALU.add)
            c_bev.inc(VE.scalar_tensor_tensor(out=cv0P[:, 0:2048], in0=rawtP[:, 3:2051],
                                              scalar=convwP_sb[:, oc * 4 + 3:oc * 4 + 4],
                                              in1=cv0P[:, 0:2048], op0=ALU.mult, op1=ALU.add))
            wt(SC, c_bev)
            c_bev.inc(SC.activation(out=siltileP[:, 0:2048], in_=cv0P[:, 0:2048], func=AF.Silu,
                                    bias=convbP_sb[:, oc:oc + 1]))
            wt(SY, c_bev)
            c_bst.dma(SY.dma_start(out=kvP_scr[oc * 128:(oc + 1) * 128, hb:hb + 2048],
                                   in_=siltileP[:, 0:2048]))
    wt(GP, c_amm)
    c_lg.dma(GP.dma_start(out=wba_sb[:, :].rearrange("p (k c) -> p k c", k=KE),
                          in_=wba[:, :].rearrange("(k p) c -> p k c", p=128)))
    wt(VE, c_bst)
    wt(VE, c_bev)
    c_sc.inc(VE.memset(A2F[:, 0:2048], 0.0))

    decAP = A3[:, 0:512]
    vi_pairP = A3[:, 512:2560]
    kw_pairP = A3[:, 2560:4608]
    kc_pairP = A3[:, 4608:5632]
    for st in range(8):
        t0 = st * SUB
        wt(SY, c_dst)
        wt(SY, c_bst)
        wt(SY, c_bmm)
        wt(SY, c_dc)
        c_d.dma(SY.dma_start(out=A1[:, 16 * SUB:24 * SUB].rearrange("p (s t) -> p s t", s=8),
                             in_=kvP_scr[0:1024, t0:t0 + SUB].rearrange("(s p) t -> p s t", p=128)))
        c_d.dma(SY.dma_start(out=A1[:, 32 * SUB:48 * SUB].rearrange("p (s t) -> p s t", s=16),
                             in_=kvP_scr[1024:3072, t0:t0 + SUB].rearrange("(s p) t -> p s t", p=128)))
        for slot in range(16, 24):
            tile = qkP(slot)
            wt(SC, c_d)
            wt(SC, c_dc)
            wt(SC, c_dmm)
            wt(SC, c_amm)
            wt(SC, c_ac)
            c_dc.inc(SC.activation(out=sqv[:, 0:SUB], in_=tile, func=AF.Square))
            wt(TE, c_dc)
            c_dmm.inc(TE.matmul(psX[:, 0:SUB], onesb[:, 0:128], sqv[:, 0:SUB], start=True, stop=True))
            wt(SC, c_dmm)
            SC.activation(out=varb[:, 0:SUB], in_=psX[:, 0:SUB], func=AF.Ln, bias=epsl2[:, 0:1])
            c_dc.inc(SC.activation(out=varb[:, 0:SUB], in_=varb[:, 0:SUB], func=AF.Exp, scale=-0.5))
            wt(VE, c_dc)
            c_dc.inc(VE.tensor_tensor(out=tile, in0=tile, in1=varb[:, 0:SUB], op=ALU.mult))
        wt(GP, c_bst)
        wt(GP, c_dc)
        c_dg.dma(GP.dma_start(out=gchr[:, 0:1024].rearrange("c (h t) -> c h t", h=16),
                              in_=gstatP_scr[0:16, t0:t0 + SUB].rearrange("h (c t) -> c h t", c=NCS)))
        c_dg.dma(GP.dma_start(out=bch[:, 0:1024].rearrange("c (h t) -> c h t", h=16),
                              in_=gstatP_scr[16:32, t0:t0 + SUB].rearrange("h (c t) -> c h t", c=NCS)))
        wt(VE, c_d)
        wt(VE, c_dg)
        for h in range(16):
            c_dc.inc(VE.tensor_tensor_scan(out=gch[:, h * C:(h + 1) * C], data0=gchr[:, h * C:(h + 1) * C],
                                           data1=zeros_ch[:, 0:C], initial=0.0, op0=ALU.add, op1=ALU.add))
        wt(SC, c_dc)
        for h in range(16):
            c_dc.inc(SC.activation(out=kwch[:, h * C:(h + 1) * C], in_=gchr[:, h * C:(h + 1) * C],
                                   func=AF.Exp, bias=gchr[:, h * C + C - 1:h * C + C], scale=-1.0))
        for (dstt, srct, fp32) in ((bT2, bch, False), (kwT2, kwch, False), (grT2, gchr, True)):
            for h in range(16):
                wt(TE, c_dc)
                wt(TE, c_dmm)
                if fp32:
                    c_dmm.inc(TE.transpose(psX[0:C, 0:NCS], srct[:, h * C:(h + 1) * C], identf[0:NCS, 0:NCS]))
                else:
                    c_dmm.inc(TE.transpose(psB16[0:C, 0:NCS], srct[:, h * C:(h + 1) * C], identb[0:NCS, 0:NCS]))
                wt(VE, c_dmm)
                c_dc.inc(VE.tensor_copy(out=dstt[(h % 2) * 64:(h % 2) * 64 + 64,
                                                 (h // 2) * NCS:(h // 2 + 1) * NCS],
                                        in_=(psX[0:C, 0:NCS] if fp32 else psB16[0:C, 0:NCS])))
        wt(SC, c_dc)
        c_dc.inc(SC.activation(out=grT2[:, 0:64], in_=grT2[:, 0:64], func=AF.Exp))
        wt(VE, c_dc)
        VE.tensor_tensor(out=bgT2[:, 0:64], in0=bT2[:, 0:64], in1=grT2[:, 0:64], op=ALU.mult)
        c_dc.inc(VE.tensor_scalar(out=nbT2[:, 0:64], in0=bT2[:, 0:64], scalar1=-1.0, scalar2=None, op0=ALU.mult))
        wt(TE, c_dc)
        c_dmm.inc(TE.matmul(psX[0:1, 0:64], e63c[0:64, 0:1], grT2[0:64, 0:64], start=True, stop=True))
        c_dmm.inc(TE.matmul(psW[0:1, 0:64], e63c[64:128, 0:1], grT2[64:128, 0:64], start=True, stop=True))
        wt(VE, c_dmm)
        VE.tensor_copy(out=lrow[0:1, 0:128].rearrange("o (h c) -> o h c", c=NCS)[:, 0::2, :],
                       in_=psX[0:1, 0:64].rearrange("o (h c) -> o h c", c=NCS))
        c_dc.inc(VE.tensor_copy(out=lrow[0:1, 0:128].rearrange("o (h c) -> o h c", c=NCS)[:, 1::2, :],
                                in_=psW[0:1, 0:64].rearrange("o (h c) -> o h c", c=NCS)))
        wt(TE, c_dc)
        c_dmm.inc(TE.matmul(psW[:, 0:128], ones1b[:, 0:128], lrow[0:1, 0:128], start=True, stop=True))
        wt(VE, c_dmm)
        c_dc.inc(VE.tensor_copy(
            out=lamS[:, :].rearrange("p (h s c) -> p h s c", s=8, c=NCS)[:, :, st, :],
            in_=psW[:, 0:128].rearrange("p (h c) -> p h c", c=NCS)))

        for hp in range(8):
            wt(VE, c_dst)
            h0, h1 = 2 * hp, 2 * hp + 1
            for hh, h in ((0, h0), (1, h1)):
                tp = None if hh == 0 else (0, 64)
                wt(VE, c_dmm)
                c_dc.inc(VE.tensor_tensor(out=dp[:, :].rearrange("p (r t) -> p r t", r=NCS),
                                          in0=gch[:, h * C:(h + 1) * C].unsqueeze(1).broadcast_to([NCS, NCS, C]),
                                          in1=sel8[:, :].rearrange("p (r t) -> p r t", r=NCS),
                                          op=ALU.mult))
                wt(TE, c_dc)
                wt(TE, c_bev)
                c_dmm.inc(TE.matmul(psT[hh * 64:hh * 64 + 64, 0:NCS * C], gch[:, h * C:(h + 1) * C],
                                    sel8[:, :], start=True, stop=False, tile_position=tp))
                c_dmm.inc(TE.matmul(psT[hh * 64:hh * 64 + 64, 0:NCS * C], nones8[:, 0:64],
                                    dp[:, :], start=False, stop=True, tile_position=tp))
            wt(VE, c_dmm)
            c_dc.inc(VE.tensor_tensor(out=psT[:, 0:NCS * C].rearrange("p (r t) -> p r t", r=NCS),
                                      in0=psT[:, 0:NCS * C].rearrange("p (r t) -> p r t", r=NCS),
                                      in1=bigLS[:, :].unsqueeze(1).broadcast_to([128, NCS, C]),
                                      op=ALU.add))
            wt(VE, c_dc)
            c_dc.inc(VE.tensor_scalar(out=psT[:, 0:NCS * C], in0=psT[:, 0:NCS * C],
                                      scalar1=-30000.0, scalar2=None, op0=ALU.add))
            wt(SC, c_dc)
            c_dc.inc(SC.activation(out=decAP[:, :], in_=psT[:, 0:NCS * C], func=AF.Exp))

            for cch in range(NCS):
                col = hp * NCS + cch
                gcol = lambda t2, hh: t2[hh * 64:hh * 64 + 64, col:col + 1]
                kn = qkP(16 + hp)[:, cch * C:(cch + 1) * C]
                wt(TE, c_dc)
                c_dmm.inc(TE.matmul(psX[0:64, 0:64], kn, kn, start=True, stop=True))
                wt(VE, c_dmm)
                for hh in range(2):
                    c_dc.inc(VE.scalar_tensor_tensor(
                        out=Ablk[hh * 64:hh * 64 + 64, hh * 64:hh * 64 + 64],
                        in0=psX[0:64, 0:64], scalar=gcol(nbT2, hh),
                        in1=decAP[hh * 64:hh * 64 + 64, cch * C:(cch + 1) * C],
                        op0=ALU.mult, op1=ALU.mult))
                if st == 0 and hp == 0 and cch == 0:
                    VE.memset(Ablk[0:64, 64:128], 0.0)
                    c_dc.inc(VE.memset(Ablk[64:128, 0:64], 0.0))
                wt(TE, c_dc)
                c_dmm.inc(TE.transpose(psB16[:, 0:128], Ablk[:, :], identb[:, 0:128]))
                wt(VE, c_dmm)
                VE.tensor_copy(out=AblkT[:], in_=psB16[:, 0:128])
                VE.tensor_add(out=Ptl[:], in0=AblkT[:], in1=identb[:, 0:128])
                VE.tensor_copy(out=Bv0[:], in_=Ablk[:])
                c_dc.inc(VE.tensor_copy(out=BvT0[:], in_=AblkT[:]))
                for j in range(1, 6):
                    wt(TE, c_dc)
                    c_dmm.inc(TE.matmul(psW[:, 0:128], BvT0[:], Bv0[:], start=True, stop=True))
                    if j < 5:
                        c_dmm.inc(TE.matmul(psW[:, 128:256], Bv0[:], BvT0[:], start=True, stop=True))
                    wt(VE, c_dmm)
                    c_dc.inc(VE.tensor_copy(out=Bv1[:], in_=psW[:, 0:128]))
                    if j < 5:
                        c_dc.inc(VE.tensor_copy(out=BvT1[:], in_=psW[:, 128:256]))
                    wt(TE, c_dc)
                    c_dmm.inc(TE.matmul(psW[:, 256:384], Bv1[:], Ptl[:], start=True, stop=True))
                    wt(VE, c_dmm)
                    VE.tensor_add(out=Ptl[:], in0=Ptl[:], in1=psW[:, 256:384])
                    VE.tensor_copy(out=Bv0[:], in_=Bv1[:])
                    if j < 5:
                        c_dc.inc(VE.tensor_copy(out=BvT0[:], in_=BvT1[:]))
                    else:
                        c_dc.inc(VE.tensor_copy(out=Bv0[:], in_=Bv1[:]))
                wt(TE, c_dc)
                c_dmm.inc(TE.transpose(psB16[0:64, 128:256], kn, identb[:, 0:128]))
                wt(VE, c_dmm)
                for hh in range(2):
                    VE.tensor_scalar(out=Rblk[hh * 64:hh * 64 + 64, 128:256], in0=psB16[0:64, 128:256],
                                     scalar1=gcol(bgT2, hh), scalar2=None, op0=ALU.mult)
                    c_dc.inc(VE.tensor_scalar(out=kw_pairP[0:64, (cch * 2 + hh) * 128:(cch * 2 + hh + 1) * 128],
                                              in0=psB16[0:64, 128:256],
                                              scalar1=gcol(kwT2, hh), scalar2=None, op0=ALU.mult))
                for hh, h in ((0, h0), (1, h1)):
                    vt = qkP(32 + h)[:, cch * C:(cch + 1) * C]
                    wt(TE, c_dc)
                    c_dmm.inc(TE.transpose(psB16[0:64, 256:384], vt, identb[:, 0:128]))
                    wt(VE, c_dmm)
                    c_dc.inc(VE.tensor_scalar(out=Rblk[hh * 64:hh * 64 + 64, 0:128], in0=psB16[0:64, 256:384],
                                              scalar1=gcol(bT2, hh), scalar2=None, op0=ALU.mult))
                wt(TE, c_dc)
                c_dmm.inc(TE.matmul(psQ[0][:, 0:256], Ptl[:], Rblk[:], start=True, stop=True))
                wt(VE, c_dmm)
                for hh in range(2):
                    c_dc.inc(VE.tensor_copy(out=vi_pairP[0:64, (cch * 2 + hh) * 128:(cch * 2 + hh + 1) * 128],
                                            in_=psQ[0][hh * 64:hh * 64 + 64, 0:128]))
                    c_dc.inc(VE.tensor_copy(out=kcdT_sb[:, :], in_=psQ[0][hh * 64:hh * 64 + 64, 128:256]))
                    wt(TE, c_dc)
                    c_dmm.inc(TE.transpose(psB16[:, 384:448], kcdT_sb[:, :], identb[0:64, 0:64]))
                    wt(VE, c_dmm)
                    c_dc.inc(VE.tensor_scalar(out=kc_pairP[:, (cch * 2 + hh) * 64:(cch * 2 + hh + 1) * 64],
                                              in0=psB16[:, 384:448], scalar1=-1.0, scalar2=None, op0=ALU.mult))
            rowp = st * 8 + hp
            wt(SY, c_dc)
            c_dst.dma(SY.dma_start(out=stash_vi[rowp:rowp + 1, :].rearrange("o (p t) -> (o p) t", p=64),
                                   in_=vi_pairP[0:64, :]))
            c_dst.dma(SY.dma_start(out=stash_kw[rowp:rowp + 1, :].rearrange("o (p t) -> (o p) t", p=64),
                                   in_=kw_pairP[0:64, :]))
            c_dst.dma(SY.dma_start(out=stash_kc[rowp:rowp + 1, :].rearrange("o (p t) -> (o p) t", p=128),
                                   in_=kc_pairP[:, :]))
    prefP_dst = c_dst.n

    sc_viP = A3[:, 8192:10240]
    sc_kwP = A3[:, 0:2048]
    sc_kcP = A3[:, 4096:5120]
    for cg in range(64):
        stq, cch = cg // 8, cg % 8
        wt(GP, c_sc)
        wt(GP, c_dst, prefP_dst)
        r0 = stq * 8
        srcP = lambda t, p: t[r0:r0 + 8, :].rearrange("r (p c h t) -> p c r h t", p=p, c=NCS, h=2)[:, cch, :, :, :]
        c_sl.dma(GP.dma_start(out=sc_viP[0:64, :].rearrange("p (r h t) -> p r h t", r=8, h=2),
                              in_=srcP(stash_vi, 64)))
        c_sl.dma(GP.dma_start(out=sc_kwP[0:64, :].rearrange("p (r h t) -> p r h t", r=8, h=2),
                              in_=srcP(stash_kw, 64)))
        c_sl.dma(GP.dma_start(out=sc_kcP[:, :].rearrange("p (r h t) -> p r h t", r=8, h=2),
                              in_=srcP(stash_kc, 128)))
        vnew8P = A3[0:64, 14336:15360]
        for bb in range(2):
            hs = list(range(bb * 8, bb * 8 + 8))
            wt(TE, c_sl)
            wt(TE, c_sc)
            for i, h in enumerate(hs):
                pv = (psQ[2] if i < 4 else psQ[3])[0:64, (i % 4) * 128:(i % 4) * 128 + 128]
                c_smm.inc(TE.matmul(pv, identb[0:64, 0:64],
                                    sc_viP[0:64, h * 128:(h + 1) * 128],
                                    start=(i % 4 == 0), stop=False))
                c_smm.inc(TE.matmul(pv, sc_kcP[:, h * 64:(h + 1) * 64],
                                    Sbf[:, h * 128:(h + 1) * 128],
                                    start=False, stop=(i % 4 == 3)))
            wt(SC, c_smm)
            SC.activation(out=vnew8P[:, 0:512], in_=psQ[2][0:64, 0:512], func=AF.Copy)
            c_sc.inc(SC.activation(out=vnew8P[:, 512:1024], in_=psQ[3][0:64, 0:512], func=AF.Copy))
            wt(TE, c_sc)
            for i, h in enumerate(hs):
                vsl = vnew8P[0:64, i * 128:(i + 1) * 128]
                ps = (psT if i < 4 else psW)[:, (i % 4) * 128:(i % 4) * 128 + 128]
                c_smm.inc(TE.matmul(ps, sc_kwP[0:64, h * 128:(h + 1) * 128], vsl,
                                    start=(i % 4 == 0), stop=(i % 4 == 3)))
            wt(VE, c_smm)
            for i, h in enumerate(hs):
                ps = (psT if i < 4 else psW)[:, (i % 4) * 128:(i % 4) * 128 + 128]
                c_sc.inc(VE.scalar_tensor_tensor(out=A2F[:, h * 128:(h + 1) * 128],
                                                 in0=A2F[:, h * 128:(h + 1) * 128],
                                                 scalar=lamS[:, h * 64 + cg:h * 64 + cg + 1],
                                                 in1=ps, op0=ALU.mult, op1=ALU.add))
            wt(GP, c_sc)
            c_sc.inc(GP.tensor_copy(out=Sbf[:, bb * 1024:(bb + 1) * 1024],
                                    in_=A2F[:, bb * 1024:(bb + 1) * 1024]))
    prefP_sc = c_sc.snap()

    wt(GP, c_sc, prefP_sc)
    iffp = []
    rec.items["gpsimd"].append(("iff", ctrli_sb[0:1, 7:8], iffp))
    c_pp.dma(GP.dma_start(out=mbox_s[:, 0:2048], in_=A2F[:, 0:2048]))
    rec.items["gpsimd"].append(("wait", c_pp.sem_for("gpsimd"), c_pp.n_of("gpsimd")))
    c_pp.dma(GP.dma_start(out=mbox_f[0:1, 1:2], in_=ctrli[0:1, 4:5]))
    rec.items["gpsimd"].append(("wait", c_pp.sem_for("gpsimd"), c_pp.n_of("gpsimd")))
    rec.items["gpsimd"].append(("endif", iffp))
    wt(SY, c_bev)
    wt(SY, c_ac)
    iffr = []
    rec.items["sync"].append(("iff", ctrli_sb[0:1, 5:6], iffr))
    rec.items["sync"].append(("poll", c_pp.sem_for("sync"), flg[0:1, 1:2], mbox_f[0:1, 1:2], ctrli_sb[0:1, 4:5], 0))
    c_ch.dma(SY.dma_start(out=A2F[:, 2048:4096], in_=mbox_s[:, 0:2048]))
    rec.items["sync"].append(("endif", iffr))
    wt(VE, c_sc, prefP_sc)
    wtreg(VE, c_ch, ctrli_sb[0:1, 6:7])
    VE.tensor_scalar(out=Ssb[:, 0:2048], in0=A2F[:, 0:2048], scalar1=ctrl_sb[:, 1:2],
                     scalar2=None, op0=ALU.mult)
    c_sc.inc(VE.tensor_scalar(out=Ssb[:, 2048:4096], in0=A2F[:, 2048:4096], scalar1=ctrl_sb[:, 1:2],
                              scalar2=None, op0=ALU.mult))

    for beng in (SY, GP, VE, SC, TE):
        for bctr in (c_ld, c_lg, c_x, c_amm, c_ac, c_w, c_w2, c_bmm, c_bev, c_bst,
                     c_d, c_dg, c_dc, c_dmm, c_dst, c_sl, c_smm, c_sc):
            wt(beng, bctr)

    xnsp = lambda k: A1[:, k * (SPAN + 3):(k + 1) * (SPAN + 3)]
    for st in range(NSUB):
        t0 = st * SUB
        x0 = 0 if st == 0 else 3 + t0
        cols = SUB + 3 if st == 0 else SUB
        parts = [(0, 512)] + ([(512, cols - 512)] if cols > 512 else [])
        for k in range(KE):
            wt(SY, c_ac)
            c_x.dma(SY.dma_start(out=xf[:, 0:cols], in_=xT[k * 128:(k + 1) * 128, x0:x0 + cols]))
            wt(SC, c_x)
            wt(SC, c_amm)
            c_ac.inc(SC.activation(out=sqv[:, 0:cols], in_=xf[:, 0:cols], func=AF.Square))
            wt(VE, c_x)
            wt(VE, c_amm)
            VE.tensor_copy(out=xbv[:, 0:cols], in_=xf[:, 0:cols])
            c_ac.inc(VE.tensor_copy(out=xnsp(k)[:, x0:x0 + cols], in_=xf[:, 0:cols]))
            wt(TE, c_ac)
            for pi, (o, w) in enumerate(parts):
                pa = psX[:, 0:w] if pi == 0 else psT[:, 0:w]
                pb = psW[:, 0:w] if pi == 0 else psT[:, 128:128 + w]
                c_amm.inc(TE.matmul(pa, onesb[:, 0:128], xbv[:, o:o + w],
                                    start=(k == 0), stop=(k == KE - 1 and pi == 0)))
                c_amm.inc(TE.matmul(pb, onesb[:, 0:128], sqv[:, o:o + w],
                                    start=(k == 0 and pi == 0), stop=(k == KE - 1)))
        wt(VE, c_amm)
        for pi, (o, w) in enumerate(parts):
            pa = psX[:, 0:w] if pi == 0 else psT[:, 0:w]
            pb = psW[:, 0:w] if pi == 0 else psT[:, 128:128 + w]
            VE.tensor_scalar(out=meanb[:, o:o + w], in0=pa, scalar1=1.0 / E, scalar2=None, op0=ALU.mult)
            VE.tensor_tensor(out=m2b[:, o:o + w], in0=meanb[:, o:o + w], in1=meanb[:, o:o + w], op=ALU.mult)
            c_ac.inc(VE.scalar_tensor_tensor(out=varb[:, o:o + w], in0=pb, scalar=1.0 / E,
                                             in1=m2b[:, o:o + w], op0=ALU.mult, op1=ALU.subtract))
        wt(SC, c_ac)
        SC.activation(out=varb[:, 0:cols], in_=varb[:, 0:cols], func=AF.Ln, bias=epsln[:, 0:1])
        c_ac.inc(SC.activation(out=varb[:, 0:cols], in_=varb[:, 0:cols], func=AF.Exp, scale=-0.5))
        for k in range(KE):
            sl = xnsp(k)[:, x0:x0 + cols]
            wt(VE, c_ac)
            VE.tensor_tensor(out=sl, in0=sl, in1=meanb[:, 0:cols], op=ALU.subtract)
            c_ac.inc(VE.tensor_tensor(out=sl, in0=sl, in1=varb[:, 0:cols], op=ALU.mult))
            wt(SY, c_ac)
            c_bst.dma(SY.dma_start(out=xn_scr[k * 128:(k + 1) * 128, t0:t0 + SUB],
                                   in_=xnsp(k)[:, 3 + t0:3 + t0 + SUB]))
        for k in range(KE):
            wt(TE, c_ac)
            c_amm.inc(TE.matmul(psX[0:64, 0:SUB], wba_sb[:, k * 64:(k + 1) * 64],
                                xnsp(k)[:, 3 + t0:3 + t0 + SUB],
                                start=(k == 0), stop=(k == KE - 1)))
        wt(SC, c_amm)
        SC.activation(out=b_sp[:, t0:t0 + SUB], in_=psX[0:32, 0:SUB], func=AF.Copy)
        c_ac.inc(SC.activation(out=a_sp[:, t0:t0 + SUB], in_=psX[32:64, 0:SUB], func=AF.Copy))

    wt(VE, c_ac)
    c_ac.inc(VE.tensor_scalar(out=gtmp[:], in0=a_sp[:], scalar1=dtb_sb[0:32, 0:1],
                              scalar2=None, op0=ALU.add))
    wt(SC, c_ac)
    c_ac.inc(SC.activation(out=b_sp[:], in_=b_sp[:], func=AF.Sigmoid))
    wt(SC, c_ac)
    c_ac.inc(SC.activation(out=a_sp[:], in_=gtmp[:], func=AF.Abs))
    SC.activation(out=a_sp[:], in_=a_sp[:], func=AF.Exp, scale=-1.0)
    c_ac.inc(SC.activation(out=a_sp[:], in_=a_sp[:], func=AF.Ln, bias=onecol[0:32, 0:1]))
    wt(VE, c_ac)
    VE.tensor_scalar(out=gtmp[:], in0=gtmp[:], scalar1=0.0, scalar2=None, op0=ALU.max)
    VE.tensor_tensor(out=a_sp[:], in0=a_sp[:], in1=gtmp[:], op=ALU.add)
    c_ac.inc(VE.tensor_scalar(out=a_sp[:], in0=a_sp[:],
                              scalar1=nea_sb[0:32, 0:1], scalar2=None, op0=ALU.mult))
    wt(SY, c_ac)
    c_bst.dma(SY.dma_start(out=gstat_scr[0:32, :], in_=a_sp[:]))
    c_bst.dma(SY.dma_start(out=gstat_scr[32:64, :], in_=b_sp[:]))
    phA_bst = c_bst.n
    dbg_dump(0, xnsp(0)[:, 3:3 + 512], c_ac)
    dbg_dump(1, a_sp[:, 0:512], c_ac)
    dbg_dump(2, b_sp[:, 0:512], c_ac)

    wq = [A3[:, i * 2048:(i + 1) * 2048] for i in range(2)]
    siltile = A3[:, 4096:6144]
    rawt = A2F[:, 0:2051]
    cv0 = A2F[:, 2051:4099]
    NT = [(0, 512), (512, 512), (1024, 512), (1536, 512), (2048, 3)]
    bmm_hist = []
    for oc in range(96):
        buf = wq[oc % 2]
        if oc >= 2:
            wt(GP, c_bmm, bmm_hist[oc - 2])
        cwp(oc).dma(GP.dma_start(out=buf[:, :].rearrange("p (k c) -> p k c", k=KE),
                                 in_=wqkvz[:, oc * 128:(oc + 1) * 128].rearrange("(k p) c -> p k c", p=128)))
        wt(TE, cwp(oc))
        wt(TE, c_bev)
        for (o, w) in NT:
            for k in range(KE):
                ps = psQ[o // 512][:, 0:512] if w == 512 else psT[:, 0:3]
                c_bmm.inc(TE.matmul(ps, buf[:, k * 128:(k + 1) * 128], xnsp(k)[:, o:o + w],
                                    start=(k == 0), stop=(k == KE - 1)))
        bmm_hist.append(c_bmm.n)
        wt(SC, c_bmm)
        wt(SC, c_bev)
        wt(SC, c_bst)
        if oc < 64:
            for (o, w) in NT:
                ps = psQ[o // 512][:, 0:w] if w == 512 else psT[:, 0:3]
                c_bev.inc(SC.activation(out=rawt[:, o:o + w], in_=ps, func=AF.Copy))
            wt(VE, c_bev)
            VE.tensor_scalar(out=cv0[:, 0:2048], in0=rawt[:, 0:2048],
                             scalar1=convw_sb[:, oc * 4:oc * 4 + 1], scalar2=None, op0=ALU.mult)
            VE.scalar_tensor_tensor(out=cv0[:, 0:2048], in0=rawt[:, 1:2049],
                                    scalar=convw_sb[:, oc * 4 + 1:oc * 4 + 2],
                                    in1=cv0[:, 0:2048], op0=ALU.mult, op1=ALU.add)
            VE.scalar_tensor_tensor(out=cv0[:, 0:2048], in0=rawt[:, 2:2050],
                                    scalar=convw_sb[:, oc * 4 + 2:oc * 4 + 3],
                                    in1=cv0[:, 0:2048], op0=ALU.mult, op1=ALU.add)
            c_bev.inc(VE.scalar_tensor_tensor(out=cv0[:, 0:2048], in0=rawt[:, 3:2051],
                                              scalar=convw_sb[:, oc * 4 + 3:oc * 4 + 4],
                                              in1=cv0[:, 0:2048], op0=ALU.mult, op1=ALU.add))
            wt(SC, c_bev)
            c_bev.inc(SC.activation(out=siltile[:, 0:2048], in_=cv0[:, 0:2048], func=AF.Silu,
                                    bias=convb_sb[:, oc:oc + 1]))
            wt(SY, c_bev)
            c_bst.dma(SY.dma_start(out=qkv_scr[oc * 128:(oc + 1) * 128, :], in_=siltile[:, 0:2048]))
            if oc == 0:
                dbg_dump(3, rawt[:, 0:512], c_bev)
                dbg_dump(4, siltile[:, 0:512], c_bev)
        else:
            for (o, w) in NT:
                if o == 0:
                    c_bev.inc(SC.activation(out=siltile[:, 0:509], in_=psQ[0][:, 3:512], func=AF.Copy))
                elif w == 512:
                    c_bev.inc(SC.activation(out=siltile[:, o - 3:o + 509], in_=psQ[o // 512][:, 0:512], func=AF.Copy))
                else:
                    c_bev.inc(SC.activation(out=siltile[:, 2045:2048], in_=psT[:, 0:3], func=AF.Copy))
            wt(SY, c_bev)
            c_bst.dma(SY.dma_start(out=z_scr[(oc - 64) * 128:(oc - 63) * 128, :], in_=siltile[:, 0:2048]))
    phB_bst = c_bst.n

    qk = lambda slot: A1[:, slot * SUB:(slot + 1) * SUB]
    decA_p = A3[:, 0:512]
    vi_pair = A3[:, 512:2560]
    kw_pair = A3[:, 2560:4608]
    kc_pair = A3[:, 4608:5632]
    at_pair = A3[:, 5632:6656]
    qg_pair = A3[:, 6656:7680]
    qgb = A3[:, 7680:8192]
    for st in range(NSUB):
        t0 = st * SUB
        wt(SY, c_dst)
        wt(SY, c_bst, phB_bst)
        wt(SY, c_dc)
        c_d.dma(SY.dma_start(out=A1[:, 0:64 * SUB].rearrange("p (s t) -> p s t", s=64),
                             in_=qkv_scr[:, t0:t0 + SUB].rearrange("(s p) t -> p s t", p=128)))
        for sl in range(32):
            tile = qk(sl)
            wt(SC, c_d)
            wt(SC, c_dc)
            wt(SC, c_dmm)
            c_dc.inc(SC.activation(out=sqv[:, 0:SUB], in_=tile, func=AF.Square))
            wt(TE, c_dc)
            c_dmm.inc(TE.matmul(psX[:, 0:SUB], onesb[:, 0:128], sqv[:, 0:SUB], start=True, stop=True))
            wt(SC, c_dmm)
            SC.activation(out=varb[:, 0:SUB], in_=psX[:, 0:SUB], func=AF.Ln, bias=epsl2[:, 0:1])
            if sl < 16:
                c_dc.inc(SC.activation(out=varb[:, 0:SUB], in_=varb[:, 0:SUB], func=AF.Exp,
                                       scale=-0.5, bias=lnq[:, 0:1]))
            else:
                c_dc.inc(SC.activation(out=varb[:, 0:SUB], in_=varb[:, 0:SUB], func=AF.Exp, scale=-0.5))
            wt(VE, c_dc)
            c_dc.inc(VE.tensor_tensor(out=tile, in0=tile, in1=varb[:, 0:SUB], op=ALU.mult))
        wt(GP, c_bst, phA_bst)
        wt(GP, c_dc)
        c_dg.dma(GP.dma_start(out=gchr[:, :].rearrange("c (h t) -> c h t", h=VH),
                              in_=gstat_scr[0:32, t0:t0 + SUB].rearrange("h (c t) -> c h t", c=NCS)))
        c_dg.dma(GP.dma_start(out=bch[:, :].rearrange("c (h t) -> c h t", h=VH),
                              in_=gstat_scr[32:64, t0:t0 + SUB].rearrange("h (c t) -> c h t", c=NCS)))
        wt(VE, c_d)
        wt(VE, c_dg)
        for h in range(VH):
            c_dc.inc(VE.tensor_tensor_scan(out=gch[:, h * C:(h + 1) * C], data0=gchr[:, h * C:(h + 1) * C],
                                           data1=zeros_ch[:, 0:C], initial=0.0, op0=ALU.add, op1=ALU.add))
        wt(SC, c_dc)
        for h in range(VH):
            c_dc.inc(SC.activation(out=kwch[:, h * C:(h + 1) * C], in_=gchr[:, h * C:(h + 1) * C],
                                   func=AF.Exp, bias=gchr[:, h * C + C - 1:h * C + C], scale=-1.0))
        for (dst, srct, fp32) in ((bT2, bch, False), (kwT2, kwch, False), (grT2, gchr, True)):
            for h in range(VH):
                wt(TE, c_dc)
                wt(TE, c_dmm)
                if fp32:
                    c_dmm.inc(TE.transpose(psX[0:C, 0:NCS], srct[:, h * C:(h + 1) * C], identf[0:NCS, 0:NCS]))
                else:
                    c_dmm.inc(TE.transpose(psB16[0:C, 0:NCS], srct[:, h * C:(h + 1) * C], identb[0:NCS, 0:NCS]))
                wt(VE, c_dmm)
                c_dc.inc(VE.tensor_copy(out=dst[(h % 2) * 64:(h % 2) * 64 + 64,
                                                (h // 2) * NCS:(h // 2 + 1) * NCS],
                                        in_=(psX[0:C, 0:NCS] if fp32 else psB16[0:C, 0:NCS])))
        wt(SC, c_dc)
        c_dc.inc(SC.activation(out=grT2[:], in_=grT2[:], func=AF.Exp))
        wt(VE, c_dc)
        VE.tensor_tensor(out=bgT2[:], in0=bT2[:], in1=grT2[:], op=ALU.mult)
        c_dc.inc(VE.tensor_scalar(out=nbT2[:], in0=bT2[:], scalar1=-1.0, scalar2=None, op0=ALU.mult))
        wt(TE, c_dc)
        c_dmm.inc(TE.matmul(psX[0:1, 0:KH * NCS], e63c[0:64, 0:1], grT2[0:64, :], start=True, stop=True))
        c_dmm.inc(TE.matmul(psW[0:1, 0:KH * NCS], e63c[64:128, 0:1], grT2[64:128, :], start=True, stop=True))
        wt(VE, c_dmm)
        VE.tensor_copy(out=lrow[0:1, :].rearrange("o (h c) -> o h c", c=NCS)[:, 0::2, :],
                       in_=psX[0:1, 0:KH * NCS].rearrange("o (h c) -> o h c", c=NCS))
        c_dc.inc(VE.tensor_copy(out=lrow[0:1, :].rearrange("o (h c) -> o h c", c=NCS)[:, 1::2, :],
                                in_=psW[0:1, 0:KH * NCS].rearrange("o (h c) -> o h c", c=NCS)))
        wt(TE, c_dc)
        c_dmm.inc(TE.matmul(psW[:, 0:VH * NCS], ones1b[:, 0:128], lrow[:, :], start=True, stop=True))
        wt(VE, c_dmm)
        c_dc.inc(VE.tensor_copy(
            out=lamS[:, :].rearrange("p (h s c) -> p h s c", s=NSUB, c=NCS)[:, :, st, :],
            in_=psW[:, 0:VH * NCS].rearrange("p (h c) -> p h c", c=NCS)))

        for hp in range(KH):
            wt(VE, c_dst)
            h0, h1 = 2 * hp, 2 * hp + 1
            for hh, h in ((0, h0), (1, h1)):
                tp = None if hh == 0 else (0, 64)
                wt(VE, c_dmm)
                c_dc.inc(VE.tensor_tensor(out=dp[:, :].rearrange("p (r t) -> p r t", r=NCS),
                                          in0=gch[:, h * C:(h + 1) * C].unsqueeze(1).broadcast_to([NCS, NCS, C]),
                                          in1=sel8[:, :].rearrange("p (r t) -> p r t", r=NCS),
                                          op=ALU.mult))
                wt(TE, c_dc)
                wt(TE, c_bev)
                c_dmm.inc(TE.matmul(psT[hh * 64:hh * 64 + 64, 0:NCS * C], gch[:, h * C:(h + 1) * C],
                                    sel8[:, :], start=True, stop=False, tile_position=tp))
                c_dmm.inc(TE.matmul(psT[hh * 64:hh * 64 + 64, 0:NCS * C], nones8[:, 0:64],
                                    dp[:, :], start=False, stop=True, tile_position=tp))
            wt(VE, c_dmm)
            c_dc.inc(VE.tensor_tensor(out=psT[:, 0:NCS * C].rearrange("p (r t) -> p r t", r=NCS),
                                      in0=psT[:, 0:NCS * C].rearrange("p (r t) -> p r t", r=NCS),
                                      in1=bigLS[:, :].unsqueeze(1).broadcast_to([128, NCS, C]),
                                      op=ALU.add))
            wt(SC, c_dc)
            for hh in range(2):
                c_dc.inc(SC.activation(out=decT_t[:, hh * 512:(hh + 1) * 512],
                                       in_=psT[hh * 64:hh * 64 + 64, 0:NCS * C], func=AF.Exp, scale=-1.0))
            wt(VE, c_dc)
            c_dc.inc(VE.tensor_scalar(out=psT[:, 0:NCS * C], in0=psT[:, 0:NCS * C],
                                      scalar1=-30000.0, scalar2=None, op0=ALU.add))
            wt(SC, c_dc)
            c_dc.inc(SC.activation(out=decA_p[:, :], in_=psT[:, 0:NCS * C], func=AF.Exp))
            if st == 0 and hp == 15:
                dbg_dump(13, decT_t[:, 512:1024], c_dc)
                dbg_dump(14, gch[0:8, 30 * C:32 * C], c_dc)
                dbg_dump(9, decA_p[:, 0:512], c_dc)

            for hh, h in ((0, h0), (1, h1)):
                wt(VE, c_dc)
                c_dc.inc(VE.tensor_tensor(out=dp[:, :].rearrange("p (r t) -> p r t", r=NCS),
                                          in0=gchr[:, h * C:(h + 1) * C].unsqueeze(1).broadcast_to([NCS, NCS, C]),
                                          in1=sel8[:, :].rearrange("p (r t) -> p r t", r=NCS),
                                          op=ALU.mult))
                wt(SC, c_dc)
                c_dc.inc(SC.activation(out=dp[:, :], in_=dp[:, :], func=AF.Exp))
                wt(VE, c_dc)
                c_dc.inc(VE.tensor_tensor(out=dp[:, :], in0=dp[:, :], in1=sel8[:, :], op=ALU.mult))
                wt(TE, c_dc)
                c_dmm.inc(TE.matmul(psW[:, 0:NCS * C], nones8[:, 0:128], dp[:, :], start=True, stop=True))
                wt(VE, c_dmm)
                c_dc.inc(VE.scalar_tensor_tensor(out=qgb[:, :], in0=psW[:, 0:NCS * C], scalar=-1.0,
                                                 in1=qk(hp), op0=ALU.mult, op1=ALU.mult))
                for cch in range(NCS):
                    c_dc.inc(VE.tensor_copy(out=qg_pair[:, (cch * 2 + hh) * 64:(cch * 2 + hh + 1) * 64],
                                            in_=qgb[:, cch * C:(cch + 1) * C]))

            TS1 = dict(Ablk=Ablk, AblkT=AblkT, Bv0=Bv0, Bv1=Bv1, BvT0=BvT0,
                       BvT1=BvT1, Ptl=Ptl, Rblk=Rblk, kcdT=kcdT_sb,
                       sym=psX[0:64, 384:448], at=psX[0:64, 0:64],
                       pw=(psW[:, 0:128], psW[:, 128:256], psW[:, 256:384]),
                       py=psQ[0], tAb=psB16[:, 0:128], tkt=psB16[0:64, 128:256],
                       tvt=psB16[0:64, 256:384], tkc=psB16[:, 384:448])
            TS2 = dict(Ablk=Dmisc[:, 0:128], AblkT=Dmisc[:, 128:256],
                       Bv0=Dmisc[:, 256:384], Bv1=Dmisc[:, 384:512],
                       BvT0=Dmisc[:, 512:640], BvT1=Dmisc[:, 640:768],
                       Ptl=Dmisc[:, 768:896], Rblk=Dmisc[:, 896:1152],
                       kcdT=Dmisc[0:64, 1152:1280],
                       sym=psX[0:64, 448:512], at=psX[0:64, 64:128],
                       pw=(psX[:, 0:128], psX[:, 128:256], psX[:, 256:384]),
                       py=psQ[1], tAb=psB16[:, 512:640], tkt=psB16[0:64, 640:768],
                       tvt=psB16[0:64, 768:896], tkc=psB16[:, 896:960])

            def cchb(cch, T):
                col = hp * NCS + cch
                gcol = lambda t2, hh: t2[hh * 64:hh * 64 + 64, col:col + 1]
                kn = qk(16 + hp)[:, cch * C:(cch + 1) * C]
                qn = qk(hp)[:, cch * C:(cch + 1) * C]
                wt(TE, c_dc)
                c_dmm.inc(TE.matmul(T["sym"], kn, kn, start=True, stop=True))
                yield
                wt(VE, c_dmm)
                for hh in range(2):
                    c_dc.inc(VE.scalar_tensor_tensor(
                        out=T["Ablk"][hh * 64:hh * 64 + 64, hh * 64:hh * 64 + 64],
                        in0=T["sym"], scalar=gcol(nbT2, hh),
                        in1=decA_p[hh * 64:hh * 64 + 64, cch * C:(cch + 1) * C],
                        op0=ALU.mult, op1=ALU.mult))
                if st == 0 and hp == 0 and cch < 2:
                    VE.memset(T["Ablk"][0:64, 64:128], 0.0)
                    c_dc.inc(VE.memset(T["Ablk"][64:128, 0:64], 0.0))
                yield
                wt(TE, c_dc)
                c_dmm.inc(TE.transpose(T["tAb"], T["Ablk"][:, :], identb[:, 0:128]))
                yield
                wt(VE, c_dmm)
                VE.tensor_copy(out=T["AblkT"][:, :], in_=T["tAb"])
                VE.tensor_add(out=T["Ptl"][:, :], in0=T["AblkT"][:, :], in1=identb[:, 0:128])
                VE.tensor_copy(out=T["Bv0"][:, :], in_=T["Ablk"][:, :])
                c_dc.inc(VE.tensor_copy(out=T["BvT0"][:, :], in_=T["AblkT"][:, :]))
                yield
                for j in range(1, 6):
                    wt(TE, c_dc)
                    c_dmm.inc(TE.matmul(T["pw"][0], T["BvT0"][:, :], T["Bv0"][:, :], start=True, stop=True))
                    if j < 5:
                        c_dmm.inc(TE.matmul(T["pw"][1], T["Bv0"][:, :], T["BvT0"][:, :], start=True, stop=True))
                    yield
                    wt(VE, c_dmm)
                    c_dc.inc(VE.tensor_copy(out=T["Bv1"][:, :], in_=T["pw"][0]))
                    if j < 5:
                        c_dc.inc(VE.tensor_copy(out=T["BvT1"][:, :], in_=T["pw"][1]))
                    yield
                    wt(TE, c_dc)
                    c_dmm.inc(TE.matmul(T["pw"][2], T["Bv1"][:, :], T["Ptl"][:, :], start=True, stop=True))
                    yield
                    wt(VE, c_dmm)
                    VE.tensor_add(out=T["Ptl"][:, :], in0=T["Ptl"][:, :], in1=T["pw"][2])
                    VE.tensor_copy(out=T["Bv0"][:, :], in_=T["Bv1"][:, :])
                    if j < 5:
                        c_dc.inc(VE.tensor_copy(out=T["BvT0"][:, :], in_=T["BvT1"][:, :]))
                    else:
                        c_dc.inc(VE.tensor_copy(out=T["Bv0"][:, :], in_=T["Bv1"][:, :]))
                    yield
                wt(TE, c_dc)
                c_dmm.inc(TE.transpose(T["tkt"], kn, identb[:, 0:128]))
                yield
                wt(VE, c_dmm)
                for hh in range(2):
                    VE.tensor_scalar(out=T["Rblk"][hh * 64:hh * 64 + 64, 128:256], in0=T["tkt"],
                                     scalar1=gcol(bgT2, hh), scalar2=None, op0=ALU.mult)
                    c_dc.inc(VE.tensor_scalar(out=kw_pair[0:64, (cch * 2 + hh) * 128:(cch * 2 + hh + 1) * 128],
                                              in0=T["tkt"],
                                              scalar1=gcol(kwT2, hh), scalar2=None, op0=ALU.mult))
                yield
                for hh, h in ((0, h0), (1, h1)):
                    vt = qk(32 + h)[:, cch * C:(cch + 1) * C]
                    wt(TE, c_dc)
                    c_dmm.inc(TE.transpose(T["tvt"], vt, identb[:, 0:128]))
                    yield
                    wt(VE, c_dmm)
                    c_dc.inc(VE.tensor_scalar(out=T["Rblk"][hh * 64:hh * 64 + 64, 0:128], in0=T["tvt"],
                                              scalar1=gcol(bT2, hh), scalar2=None, op0=ALU.mult))
                    yield
                wt(TE, c_dc)
                c_dmm.inc(TE.matmul(T["py"][:, 0:256], T["Ptl"][:, :], T["Rblk"][:, :], start=True, stop=True))
                yield
                for hh in range(2):
                    wt(VE, c_dmm)
                    c_dc.inc(VE.tensor_copy(out=vi_pair[0:64, (cch * 2 + hh) * 128:(cch * 2 + hh + 1) * 128],
                                            in_=T["py"][hh * 64:hh * 64 + 64, 0:128]))
                    c_dc.inc(VE.tensor_copy(out=T["kcdT"][:, :], in_=T["py"][hh * 64:hh * 64 + 64, 128:256]))
                    yield
                    wt(TE, c_dc)
                    c_dmm.inc(TE.transpose(T["tkc"], T["kcdT"][:, :], identb[0:64, 0:64]))
                    yield
                    wt(VE, c_dmm)
                    c_dc.inc(VE.tensor_scalar(out=kc_pair[:, (cch * 2 + hh) * 64:(cch * 2 + hh + 1) * 64],
                                              in0=T["tkc"], scalar1=-1.0, scalar2=None, op0=ALU.mult))
                    yield
                wt(TE, c_dc)
                c_dmm.inc(TE.matmul(T["at"], kn, qn, start=True, stop=True))
                yield
                wt(VE, c_dmm)
                for hh in range(2):
                    c_dc.inc(VE.tensor_tensor(out=at_pair[0:64, (cch * 2 + hh) * 64:(cch * 2 + hh + 1) * 64],
                                              in0=T["at"],
                                              in1=decT_t[:, hh * 512 + cch * C:hh * 512 + (cch + 1) * C],
                                              op=ALU.mult))

            for cc0 in range(0, NCS, 2):
                gens = [cchb(cc0, TS1), cchb(cc0 + 1, TS2)]
                alive = [True, True]
                while any(alive):
                    for gi, g in enumerate(gens):
                        if alive[gi]:
                            try:
                                next(g)
                            except StopIteration:
                                alive[gi] = False
            if st == 0 and hp == 0:
                dbg_dump(5, grT2[:, 0:128], c_dc)
                dbg_dump(6, vi_pair[0:64, 0:512], c_dc)
                dbg_dump(7, kc_pair[:, 0:512], c_dc)
                dbg_dump(8, at_pair[0:64, 0:512], c_dc)
            row = st * KH + hp
            wt(SY, c_dc)
            c_dst.dma(SY.dma_start(out=stash_vi[row:row + 1, :].rearrange("o (p t) -> (o p) t", p=64),
                                   in_=vi_pair[0:64, :]))
            c_dst.dma(SY.dma_start(out=stash_kw[row:row + 1, :].rearrange("o (p t) -> (o p) t", p=64),
                                   in_=kw_pair[0:64, :]))
            c_dst.dma(SY.dma_start(out=stash_kc[row:row + 1, :].rearrange("o (p t) -> (o p) t", p=128),
                                   in_=kc_pair[:, :]))
            c_dst.dma(SY.dma_start(out=stash_at[row:row + 1, :].rearrange("o (p t) -> (o p) t", p=64),
                                   in_=at_pair[0:64, :]))
            c_dst.dma(SY.dma_start(out=stash_qg[row:row + 1, :].rearrange("o (p t) -> (o p) t", p=128),
                                   in_=qg_pair[:, :]))
    prep_dst = c_dst.n
    prep_dc = c_dc.n

    wt(SY, c_ac)
    iffs = []
    rec.items["sync"].append(("iff", ctrli_sb[0:1, 1:2], iffs))
    rec.items["sync"].append(("poll", c_po.sem_for("sync"), flg[0:1, 0:1], mbox_f[0:1, 0:1], ctrli_sb[0:1, 0:1], 0))
    c_ch.dma(SY.dma_start(out=stateio[:, :], in_=mbox_s[:, :]))
    rec.items["sync"].append(("endif", iffs))
    wtreg(VE, c_ch, ctrli_sb[0:1, 2:3])
    VE.scalar_tensor_tensor(out=Ssb[:, :], in0=stateio[:, :], scalar=ctrl_sb[:, 0:1],
                            in1=Ssb[:, :], op0=ALU.mult, op1=ALU.add)
    c_sc.inc(VE.tensor_copy(out=Sbf[:, :], in_=Ssb[:, :]))

    sc_vi = A3[:, 8192:12288]
    sc_kw = A3[:, 0:4096]
    sc_kc = A3[:, 4096:6144]
    sc_at = A3[:, 6144:8192]
    sc_qg = A3[:, 12288:14336]
    for cg in range(NSC):
        st, cch = cg // NCS, cg % NCS
        wt(GP, c_sc)
        wt(GP, c_dst, prep_dst)
        r0 = st * KH
        src = lambda t, p: t[r0:r0 + KH, :].rearrange("r (p c h t) -> p c r h t", p=p, c=NCS, h=2)[:, cch, :, :, :]
        dst = lambda t, p: t[0:p, :].rearrange("p (r h t) -> p r h t", r=KH, h=2)
        c_sl.dma(GP.dma_start(out=dst(sc_vi, 64), in_=src(stash_vi, 64)))
        c_sl.dma(GP.dma_start(out=dst(sc_kw, 64), in_=src(stash_kw, 64)))
        c_sl.dma(GP.dma_start(out=dst(sc_kc, 128), in_=src(stash_kc, 128)))
        c_sl.dma(GP.dma_start(out=dst(sc_at, 64), in_=src(stash_at, 64)))
        c_sl.dma(GP.dma_start(out=dst(sc_qg, 128), in_=src(stash_qg, 128)))
        if cg == 0:
            dbg_dump(0, sc_vi[0:64, 31 * 128:32 * 128], c_sl)
            dbg_dump(1, sc_kw[0:64, 31 * 128:32 * 128], c_sl)
            dbg_dump(2, sc_kc[:, 31 * 64:32 * 64], c_sl)
            dbg_dump(5, sc_at[0:64, 31 * 64:32 * 64], c_sl)
            dbg_dump(6, sc_qg[:, 31 * 64:32 * 64], c_sl)
        vnew8 = A3[0:64, 14336:15360]
        for bb in range(4):
            hs = list(range(bb * 8, bb * 8 + 8))
            wt(TE, c_sl)
            wt(TE, c_sc)
            for i, h in enumerate(hs):
                pv = (psQ[2] if i < 4 else psQ[3])[0:64, (i % 4) * 128:(i % 4) * 128 + 128]
                c_smm.inc(TE.matmul(pv, identb[0:64, 0:64],
                                    sc_vi[0:64, h * 128:(h + 1) * 128],
                                    start=(i % 4 == 0), stop=False))
                c_smm.inc(TE.matmul(pv, sc_kc[:, h * 64:(h + 1) * 64],
                                    Sbf[:, h * 128:(h + 1) * 128],
                                    start=False, stop=(i % 4 == 3)))
            wt(SC, c_smm)
            SC.activation(out=vnew8[:, 0:512], in_=psQ[2][0:64, 0:512], func=AF.Copy)
            c_sc.inc(SC.activation(out=vnew8[:, 512:1024], in_=psQ[3][0:64, 0:512], func=AF.Copy))
            wt(TE, c_sc)
            for i, h in enumerate(hs):
                vsl = vnew8[0:64, i * 128:(i + 1) * 128]
                c_smm.inc(TE.matmul(psX[:, i * 64:i * 64 + 64], Sbf[:, h * 128:(h + 1) * 128],
                                    sc_qg[:, h * 64:(h + 1) * 64], start=(i == 0), stop=False))
                c_smm.inc(TE.matmul(psX[:, i * 64:i * 64 + 64], vsl,
                                    sc_at[0:64, h * 64:(h + 1) * 64], start=False,
                                    stop=(i == 7)))
                ps = (psT if i < 4 else psW)[:, (i % 4) * 128:(i % 4) * 128 + 128]
                c_smm.inc(TE.matmul(ps, sc_kw[0:64, h * 128:(h + 1) * 128], vsl,
                                    start=(i % 4 == 0), stop=(i % 4 == 3)))
            wt(SC, c_smm)
            wt(SC, c_dst)
            c_sc.inc(SC.activation(out=corehold[:, bb * 512:(bb + 1) * 512],
                                   in_=psX[:, 0:512], func=AF.Copy))
            wt(VE, c_smm)
            for i, h in enumerate(hs):
                ps = (psT if i < 4 else psW)[:, (i % 4) * 128:(i % 4) * 128 + 128]
                c_sc.inc(VE.scalar_tensor_tensor(out=Ssb[:, h * 128:(h + 1) * 128],
                                                 in0=Ssb[:, h * 128:(h + 1) * 128],
                                                 scalar=lamS[:, h * NSC + cg:h * NSC + cg + 1],
                                                 in1=ps, op0=ALU.mult, op1=ALU.add))
            wt(GP, c_sc)
            c_sc.inc(GP.tensor_copy(out=Sbf[:, bb * 1024:(bb + 1) * 1024],
                                    in_=Ssb[:, bb * 1024:(bb + 1) * 1024]))
            if cg == 0 and h == 31:
                dbg_dump(7, vnew_sb[:, :], c_sc)
                dbg_dump(8, Ssb[:, 31 * 128:32 * 128], c_sc)
        wt(SY, c_sc)
        c_dst.dma(SY.dma_start(out=core_scr[:, cg * C:(cg + 1) * C].rearrange("(h p) t -> p h t", p=128),
                               in_=corehold[:, :].rearrange("p (h t) -> p h t", h=VH)))
        if cg == 0:
            dbg_dump(10, lamS[:, 0:512], c_sc)
            dbg_dump(11, lamS[:, 512:1024], c_sc)
    scan_sc = c_sc.snap()
    scan_dst = c_dst.n

    iffs2 = []
    wt(GP, c_sc, scan_sc)
    rec.items["gpsimd"].append(("iff", ctrli_sb[0:1, 3:4], iffs2))
    c_po.dma(GP.dma_start(out=mbox_s[:, :], in_=Ssb[:, :]))
    rec.items["gpsimd"].append(("wait", c_po.sem_for("gpsimd"), c_po.n_of("gpsimd")))
    c_po.dma(GP.dma_start(out=mbox_f[0:1, 0:1], in_=ctrli[0:1, 0:1]))
    rec.items["gpsimd"].append(("wait", c_po.sem_for("gpsimd"), c_po.n_of("gpsimd")))
    rec.items["gpsimd"].append(("endif", iffs2))

    wt(SY, c_dst, scan_dst)
    c_f.dma(SY.dma_start(out=A1[:, 0:KE * SPAN].rearrange("p (k t) -> p k t", k=KE),
                         in_=xn_scr[:, :].rearrange("(k p) t -> p k t", p=128)))
    xnf = lambda k: A1[:, k * SPAN:(k + 1) * SPAN]
    wfb = [A3[:, 0:2048], A3[:, 2048:4096]]
    wfq = A3[:, 4096:6144]
    absw = A3[:, 6144:8192]
    s1t = A3[:, 8192:10240]
    gsum = A3[:, 10240:12288]
    gatesb = A3[:, 12288:14336]

    def quant(buf, cc_dma):
        wt(SC, cc_dma)
        wt(SC, c_fc)
        c_fc.inc(SC.activation(out=absw[:, :], in_=buf[:, :], func=AF.Abs))
        for qtr in range(4):
            wt(TE, c_fc)
            wt(TE, c_fmm)
            c_fmm.inc(TE.matmul(psW[:, 0:512], bdiagb[:, 0:128], absw[:, qtr * 512:(qtr + 1) * 512],
                                start=True, stop=True))
            wt(SC, c_fmm)
            c_fc.inc(SC.activation(out=gsum[:, qtr * 512:(qtr + 1) * 512], in_=psW[:, 0:512], func=AF.Copy))
        wt(VE, c_fc)
        VE.scalar_tensor_tensor(out=s1t[:, :], in0=gsum[:, :], scalar=QTHR / 64.0,
                                in1=buf[:, :], op0=ALU.mult, op1=ALU.is_lt)
        c_fc.inc(VE.scalar_tensor_tensor(out=absw[:, :], in0=gsum[:, :], scalar=-QTHR / 64.0,
                                         in1=buf[:, :], op0=ALU.mult, op1=ALU.is_gt))
        wt(GP, c_fc)
        c_fc.inc(GP.tensor_sub(out=s1t[:, :], in0=s1t[:, :], in1=absw[:, :]))
        wt(VE, c_fc)
        wt(VE, c_fmm)
        c_fc.inc(VE.scalar_tensor_tensor(out=wfq[:, :], in0=gsum[:, :], scalar=1.0 / 64.0,
                                         in1=s1t[:, :], op0=ALU.mult, op1=ALU.mult))

    fmm_hist = []
    wt(GP, c_sl)
    wt(GP, c_smm)
    wt(GP, c_sc)
    for j in range(64):
        for gu in range(2):
            hc = j + gu * 64
            idx = j * 2 + gu
            buf = wfb[idx % 2]
            if idx >= 2:
                wt(GP, c_fmm, fmm_hist[idx - 2])
            cwp(idx).dma(GP.dma_start(out=buf[:, :].rearrange("p (k c) -> p k c", k=KE),
                                      in_=wguT[:, hc * 128:(hc + 1) * 128].rearrange("(k p) c -> p k c", p=128)))
            quant(buf, cwp(idx))
            wt(TE, c_fc)
            wt(TE, c_fc)
            for nt in range(4):
                for k in range(KE):
                    c_fmm.inc(TE.matmul(psQ[nt][:, 0:512], wfq[:, k * 128:(k + 1) * 128],
                                        xnf(k)[:, nt * 512:(nt + 1) * 512],
                                        start=(k == 0), stop=(k == KE - 1)))
            fmm_hist.append(c_fmm.n)
            if gu == 0:
                wt(SC, c_fmm)
                for nt in range(4):
                    c_fc.inc(SC.activation(out=gatesb[:, nt * 512:(nt + 1) * 512],
                                           in_=psQ[nt][:, 0:512], func=AF.Silu))
            else:
                wt(VE, c_fmm)
                for nt in range(4):
                    c_fc.inc(VE.tensor_tensor(out=gatesb[:, nt * 512:(nt + 1) * 512],
                                              in0=gatesb[:, nt * 512:(nt + 1) * 512],
                                              in1=psQ[nt][:, 0:512], op=ALU.mult))
                wt(SY, c_fc)
                c_f.dma(SY.dma_start(out=h_scr[j * 128:(j + 1) * 128, :], in_=gatesb[:, 0:2048]))
                wt(SC, c_fc)
                c_fc.inc(SC.activation(out=s1t[:, 0:2048], in_=gatesb[:, 0:2048], func=AF.Square))
                wt(TE, c_fc)
                c_fmm.inc(TE.matmul(psT[0:1, 0:512], onesrow_b[:, 0:1], s1t[:, 0:512],
                                    start=(j == 0), stop=(j == 63)))
                c_fmm.inc(TE.matmul(psT[32:33, 0:512], onesrow_b[:, 0:1], s1t[:, 512:1024],
                                    start=(j == 0), stop=(j == 63), tile_position=(0, 32)))
                c_fmm.inc(TE.matmul(psT[64:65, 0:512], onesrow_b[:, 0:1], s1t[:, 1024:1536],
                                    start=(j == 0), stop=(j == 63), tile_position=(0, 64)))
                c_fmm.inc(TE.matmul(psX[0:1, 0:512], onesrow_b[:, 0:1], s1t[:, 1536:2048],
                                    start=(j == 0), stop=(j == 63)))
    rsrow = A2F[0:1, 4099:6147]
    wt(SC, c_fmm)
    for i, ps in enumerate((psT[0:1, 0:512], psT[32:33, 0:512], psT[64:65, 0:512], psX[0:1, 0:512])):
        SC.activation(out=rsrow[0:1, i * 512:(i + 1) * 512], in_=ps, func=AF.Ln,
                      scale=1.0 / HALF_H, bias=epsffn_c[0:1, 0:1])
    c_fc.inc(SC.activation(out=rstdrow[0:1, :], in_=rsrow[0:1, :], func=AF.Exp, scale=-0.5))

    for dc in range(KE):
        for half in range(4):
            idx = dc * 4 + half
            buf = wfb[idx % 2]
            if idx >= 2:
                wt(GP, c_f, None)
            cwp(idx).dma(GP.dma_start(out=buf[:, :].rearrange("p (k c) -> p k c", k=KE),
                                      in_=wprojT[half * 2048:(half + 1) * 2048, dc * 128:(dc + 1) * 128]
                                      .rearrange("(k p) c -> p k c", p=128)))
            quant(buf, cwp(idx))
            wt(SY, c_fc)
            c_f.dma(SY.dma_start(out=wpq_scr[half * 2048:(half + 1) * 2048, dc * 128:(dc + 1) * 128]
                                 .rearrange("(k p) c -> p k c", p=128),
                                 in_=wfq[:, :].rearrange("p (k c) -> p k c", k=KE)))
    quant_f = c_f.n

    wp64 = A3[:, 0:8192]
    gu_fmm = c_fmm.n
    for nt in range(4):
        wt(SY, c_f, quant_f)
        wt(SY, c_fmm, gu_fmm)
        wt(SY, c_emm)
        c_f.dma(SY.dma_start(out=A1[:, 0:64 * 512].rearrange("p (k t) -> p k t", k=64),
                             in_=h_scr[:, nt * 512:(nt + 1) * 512].rearrange("(k p) t -> p k t", p=128)))
        wt(TE, c_f)
        c_emm.inc(TE.matmul(psW[:, 0:512], ones1b[:, 0:128], rstdrow[0:1, nt * 512:(nt + 1) * 512],
                            start=True, stop=True))
        wt(VE, c_emm)
        c_ec.inc(VE.tensor_copy(out=varb[:, 0:512], in_=psW[:, 0:512]))
        for dc in range(KE):
            wt(GP, c_emm)
            if nt == 0 and dc == 0:
                wt(GP, c_f)
                wt(GP, c_fmm)
                wt(GP, c_fc)
            c_w.dma(GP.dma_start(out=wp64[:, :].rearrange("p (k c) -> p k c", k=64),
                                 in_=wpq_scr[:, dc * 128:(dc + 1) * 128].rearrange("(k p) c -> p k c", p=128)))
            wt(TE, c_w)
            wt(TE, c_ec)
            for k in range(64):
                c_emm.inc(TE.matmul(psQ[0][:, 0:512], wp64[:, k * 128:(k + 1) * 128],
                                    A1[:, k * 512:(k + 1) * 512], start=(k == 0), stop=(k == 63)))
            wt(VE, c_emm)
            c_ec.inc(VE.tensor_tensor(out=cv0[:, 0:512], in0=psQ[0][:, 0:512], in1=varb[:, 0:512], op=ALU.mult))
            wt(SC, c_ec)
            c_ec.inc(SC.activation(out=gatesb[:, dc * 128:(dc + 1) * 128] if False else sqv[:, 0:512],
                                   in_=cv0[:, 0:512], func=AF.Silu))
            wt(SY, c_ec)
            c_f.dma(SY.dma_start(out=g_scr[dc * 128:(dc + 1) * 128, nt * 512:(nt + 1) * 512],
                                 in_=sqv[:, 0:512]))
    ffn_f = c_f.n
    ffn_ec = c_ec.n

    woutb = [A3[:, 8192 + i * 4096:8192 + (i + 1) * 4096] for i in range(1)]
    gat = lambda h: A1[:, h * SUB:(h + 1) * SUB]
    zslot = lambda h: A1[:, (32 + h) * SUB:(33 + h) * SUB]
    for st in range(NSUB):
        t0 = st * SUB
        wt(SY, c_f, ffn_f)
        wt(SY, c_ec)
        wt(SY, c_emm)
        c_e.dma(SY.dma_start(out=A1[:, 0:32 * SUB].rearrange("p (s t) -> p s t", s=32),
                             in_=core_scr[:, t0:t0 + SUB].rearrange("(s p) t -> p s t", p=128)))
        c_e.dma(SY.dma_start(out=A1[:, 32 * SUB:64 * SUB].rearrange("p (s t) -> p s t", s=32),
                             in_=z_scr[:, t0:t0 + SUB].rearrange("(s p) t -> p s t", p=128)))
        for h in range(VH):
            wt(SC, c_e)
            wt(SC, c_ec)
            wt(SC, c_emm)
            c_ec.inc(SC.activation(out=sqv[:, 0:SUB], in_=gat(h), func=AF.Square))
            wt(TE, c_ec)
            c_emm.inc(TE.matmul(psX[:, 0:SUB], onesb[:, 0:128], sqv[:, 0:SUB], start=True, stop=True))
            wt(SC, c_emm)
            SC.activation(out=varb[:, 0:SUB], in_=psX[:, 0:SUB], func=AF.Ln, scale=1.0 / 128.0,
                          bias=epsl2[:, 0:1])
            SC.activation(out=varb[:, 0:SUB], in_=varb[:, 0:SUB], func=AF.Exp, scale=-0.5)
            c_ec.inc(SC.activation(out=m2b[:, 0:SUB], in_=zslot(h), func=AF.Silu))
            wt(VE, c_ec)
            VE.scalar_tensor_tensor(out=gat(h), in0=gat(h), scalar=normw_sb[:, 0:1],
                                    in1=varb[:, 0:SUB], op0=ALU.mult, op1=ALU.mult)
            c_ec.inc(VE.tensor_tensor(out=gat(h), in0=gat(h), in1=m2b[:, 0:SUB], op=ALU.mult))
        if st == 0:
            dbg_dump(15, A1[:, 1 * SUB:1 * SUB + 512], c_e)
            dbg_dump(4, zslot(31)[:, 0:512], c_e)
            dbg_dump(12, varb[:, 0:512], c_ec)
            if dbg_h is not None:
                wt(GP, c_dst)
                c_out.dma(GP.dma_start(out=dbg_h[:, 16 * 512:17 * 512],
                                       in_=core_scr[128:256, 0:512]))
                wt(GP, c_dst)
                c_out.dma(GP.dma_start(out=dbg_h[:, 3 * 512:4 * 512],
                                       in_=core_scr[31 * 128:32 * 128, 0:512]))
        for ec in range(KE):
            wt(GP, c_emm)
            c_w.dma(GP.dma_start(out=woutb[0][:, :].rearrange("p (k c) -> p k c", k=32),
                                 in_=wout[:, ec * 128:(ec + 1) * 128].rearrange("(k p) c -> p k c", p=128)))
            wt(TE, c_w)
            wt(TE, c_ec)
            for h in range(VH):
                c_emm.inc(TE.matmul(psQ[1][:, 0:512], woutb[0][:, h * 128:(h + 1) * 128], gat(h),
                                    start=(h == 0), stop=(h == VH - 1)))
            wt(SY, c_ec)
            c_e.dma(SY.dma_start(out=xf[:, 0:SUB], in_=xT[ec * 128:(ec + 1) * 128, 3 + t0:3 + t0 + SUB]))
            c_e.dma(SY.dma_start(out=xbv[:, 0:SUB], in_=g_scr[ec * 128:(ec + 1) * 128, t0:t0 + SUB]))
            wt(VE, c_emm)
            wt(VE, c_e)
            wt(VE, c_out)
            wt(VE, c_ec)
            c_ec.inc(VE.tensor_copy(out=m2b[:, 0:SUB], in_=psQ[1][:, 0:512]))
            if st == 0 and ec == 0:
                dbg_dump(17, A1[:, 31 * SUB:31 * SUB + 512], c_ec)
                dbg_dump(18, m2b[:, 0:512], c_ec)
                dbg_dump(19, woutb[0][:, 0:512], c_ec)
            VE.tensor_tensor(out=cv0[:, 0:SUB], in0=m2b[:, 0:SUB], in1=xbv[:, 0:SUB], op=ALU.mult)
            c_ec.inc(VE.tensor_tensor(out=cv0[:, 0:SUB], in0=cv0[:, 0:SUB], in1=xf[:, 0:SUB], op=ALU.add))
            wt(SY, c_ec)
            c_out.dma(SY.dma_start(out=out_h[ec * 128:(ec + 1) * 128, t0:t0 + SUB], in_=cv0[:, 0:SUB]))
    for _e, _s in c_out.sub.items():
        rec.items["gpsimd"].append(("wait", _s[0], _s[1]))
        rec.items["sync"].append(("wait", _s[0], _s[1]))

    with es, nc.Block() as block:
        rec.emit(block)
    return nc


_BUILT = None
_BUILT_DBG = None


def _consts_host():
    cb = np.zeros((128, 1024), np.float32)
    cb[:, 0:128] = np.eye(128)
    s = np.arange(64)
    m1 = (s[:, None] > s[None, :]).astype(np.float32) * 30000.0
    cb[0:64, 128:192] = m1
    cb[64:128, 128:192] = m1
    cb[63, 192] = 1.0
    cb[127, 192] = 1.0
    cb[:, 193] = 1.0
    cb[:, 194] = F32EPS
    for c in range(NCS):
        cb[c, 208 + c * C:208 + (c + 1) * C] = 1.0
    bd = np.zeros((128, 128), np.float32)
    bd[0:64, 0:64] = 1.0
    bd[64:128, 64:128] = 1.0
    cb[:, 720:848] = bd
    return cb


def kernel(L0_attn_hidden_states, L0_attn_w_qkvz, L0_attn_w_ba, L0_attn_conv_w,
           L0_attn_conv_b, L0_attn_A_log, L0_attn_dt_bias, L0_attn_norm_weight,
           L0_attn_w_out, L0_ffn_w_gate_up, L0_ffn_w_proj):
    global _BUILT
    x = np.asarray(L0_attn_hidden_states)
    B, L, _ = x.shape
    ncore = 8
    spans = 4
    cb = _consts_host()
    wguT = np.ascontiguousarray(np.asarray(L0_ffn_w_gate_up).T)
    wprojT = np.ascontiguousarray(np.asarray(L0_ffn_w_proj).T)
    convw_h = np.zeros((128, 256), np.float32)
    cw = np.asarray(L0_attn_conv_w)
    for ch in range(64):
        for j in range(4):
            convw_h[:, ch * 4 + j] = cw[j, ch * 128:(ch + 1) * 128]
    convb_h = np.ascontiguousarray(np.asarray(L0_attn_conv_b).reshape(64, 128).T)
    alog_h = np.asarray(L0_attn_A_log).reshape(32, 1)
    dtb_h = np.asarray(L0_attn_dt_bias).reshape(32, 1)
    normw_h = np.asarray(L0_attn_norm_weight).reshape(128, 1)
    nonce = int(np.random.randint(1, 2 ** 30))

    nonce2 = int(np.random.randint(1, 2 ** 30))
    wqkvz_h = np.asarray(L0_attn_w_qkvz)
    wba_h = np.asarray(L0_attn_w_ba)
    cb_full = np.asarray(L0_attn_conv_b)

    in_maps = []
    for core in range(ncore):
        b = core // 4
        pos = core % 4
        t0 = pos * SPAN
        xs = np.zeros((SPAN + 3, E), np.float32)
        xs[3:] = x[b, t0:t0 + SPAN]
        if t0 > 0:
            xs[0:3] = x[b, t0 - 3:t0]
        ci = np.zeros((1, 16), np.int32)
        ci[0, 0] = nonce
        ci[0, 1] = 1 if pos in (1, 3) else 0
        ci[0, 2] = 16 if pos in (1, 3) else 0
        ci[0, 3] = 1 if pos in (0, 2) else 0
        ci[0, 4] = nonce2
        ci[0, 5] = 1 if pos in (0, 2) else 0
        ci[0, 6] = 16 if pos in (0, 2) else 0
        ci[0, 7] = 1 if pos in (1, 3) else 0
        cf = np.zeros((128, 8), np.float32)
        cf[:, 0] = 1.0 if pos in (1, 3) else 0.0
        cf[:, 1] = 1.0 if pos in (0, 2) else 0.0
        hh0 = 0 if pos in (0, 2) else 16
        kh0 = hh0 // 2
        xp = np.zeros((4096 + 3, E), np.float32)
        xp[3:] = x[b, 0:4096]
        wkvp = np.zeros((E, 3072), np.float32)
        convwp = np.zeros((128, 96), np.float32)
        convbp = np.zeros((128, 24), np.float32)
        if pos in (2, 3):
            wkvp[:, 0:1024] = wqkvz_h[:, 2048 + kh0 * 128:2048 + kh0 * 128 + 1024]
            wkvp[:, 1024:3072] = wqkvz_h[:, 4096 + hh0 * 128:4096 + hh0 * 128 + 2048]
            grps = [16 + kh0 + i for i in range(8)] + [32 + hh0 + i for i in range(16)]
            for gi, g128 in enumerate(grps):
                for j in range(4):
                    convwp[:, gi * 4 + j] = cw[j, g128 * 128:(g128 + 1) * 128]
                convbp[:, gi] = cb_full[g128 * 128:(g128 + 1) * 128]
        wbap = np.zeros((E, 64), np.float32)
        wbap[:, 0:16] = wba_h[:, hh0:hh0 + 16]
        wbap[:, 32:48] = wba_h[:, 32 + hh0:32 + hh0 + 16]
        alogp = alog_h[hh0:hh0 + 16]
        dtbp = dtb_h[hh0:hh0 + 16]
        in_maps.append({
            "xT": np.ascontiguousarray(xs.T),
            "wqkvz": wqkvz_h,
            "wba": wba_h,
            "wout": np.asarray(L0_attn_w_out),
            "wguT": wguT, "wprojT": wprojT,
            "convw": convw_h, "convb": convb_h,
            "alog": alog_h, "dtb": dtb_h, "normw": normw_h,
            "consts": cb, "ctrlf": cf, "ctrli": ci,
            "xPT": np.ascontiguousarray(xp.T),
            "wkvP": wkvp, "wbaP": wbap,
            "convwP": convwp, "convbP": convbp,
            "alogP": np.ascontiguousarray(alogp), "dtbP": np.ascontiguousarray(dtbp),
        })
    import os
    want_dbg = bool(os.environ.get("KDBG"))
    global _BUILT_DBG
    if want_dbg:
        if _BUILT_DBG is None:
            _BUILT_DBG = build(dbg=True)
        nc_use = _BUILT_DBG
    else:
        if _BUILT is None:
            _BUILT = build()
        nc_use = _BUILT
    res = run_bass_kernel_spmd(nc_use, in_maps, core_ids=list(range(ncore)))
    if want_dbg:
        np.save("/tmp/kdbg.npy", np.stack([res.results[c]["dbg"] for c in range(ncore)]))
    out = np.zeros((B, L, E), np.float32)
    for core in range(ncore):
        b = core // 4
        pos = core % 4
        out[b, pos * SPAN:(pos + 1) * SPAN] = res.results[core]["o"].T
    return out
